# version 01
# baseline (speedup 1.0000x reference)
import math
import contextlib
import numpy as np
import ml_dtypes
import concourse.bass as bass
import concourse.mybir as mybir
from concourse.bass_utils import run_bass_kernel_spmd

F32 = mybir.dt.float32
BF16 = mybir.dt.bfloat16
I32 = mybir.dt.int32
AF = mybir.ActivationFunctionType
ALU = mybir.AluOpType
AX = mybir.AxisListType

T = 4096
D = 1024
L1FFN_DEPTH = 2
NEG = -3.0e38
BF = ml_dtypes.bfloat16


class Ctx:
    def __init__(self, nc, stack):
        self.nc = nc
        self.stack = stack
        self.c = {}
        self.lc = {}
        self.regs = {}
        self._sems = {}
        sb = stack.enter_context
        self.srcfm = [sb(nc.sbuf_tensor(f"srcfm{i}", [128, 4096], BF16)) for i in range(8)]
        self.wbuf = [sb(nc.sbuf_tensor(f"wbuf{i}", [128, 3072], BF16)) for i in range(8)]
        self.k_sb = sb(nc.sbuf_tensor("k_sb", [128, 4096], BF16))
        self.q_sb = sb(nc.sbuf_tensor("q_sb", [128, 4096], BF16))
        self.v_sb = sb(nc.sbuf_tensor("v_sb", [128, 2176], BF16))
        self.p_sb = sb(nc.sbuf_tensor("p_sb", [128, 1024], BF16))
        self.x0 = sb(nc.sbuf_tensor("x0", [128, 1024], F32))
        self.x1 = sb(nc.sbuf_tensor("x1", [128, 1024], F32))
        self.y0 = sb(nc.sbuf_tensor("y0", [128, 1024], F32))
        self.y1 = sb(nc.sbuf_tensor("y1", [128, 1024], F32))
        self.b0 = sb(nc.sbuf_tensor("b0", [128, 1088], BF16))
        self.b1 = sb(nc.sbuf_tensor("b1", [128, 1088], BF16))
        self.rows = sb(nc.sbuf_tensor("rows", [128, 64], F32))
        self.rows2 = sb(nc.sbuf_tensor("rows2", [128, 512], F32))
        self.rbf = sb(nc.sbuf_tensor("rbf", [128, 512], BF16))
        self.ident = sb(nc.sbuf_tensor("ident", [128, 128], BF16))
        self.onesrow = sb(nc.sbuf_tensor("onesrow", [128, 128], BF16))
        self.onescol = sb(nc.sbuf_tensor("onescol", [128, 16], BF16))
        self.cf32 = sb(nc.sbuf_tensor("cf32", [128, 16], F32))
        self.t5idx = sb(nc.sbuf_tensor("sb_t5idx", [128, 16], I32))
        self.bvec = sb(nc.sbuf_tensor("bvec", [128, 8], F32))
        self.prot = sb(nc.sbuf_tensor("prot", [64, 64], BF16))
        self.accm = sb(nc.sbuf_tensor("accm", [128, 4096], F32))
        self.sel = sb(nc.sbuf_tensor("sel", [8, 1024], BF16))
        self.wr = sb(nc.sbuf_tensor("wr", [128, 64], BF16))
        self.wbc = sb(nc.sbuf_tensor("wbc", [128, 512], BF16))
        self.maskb = sb(nc.sbuf_tensor("maskb", [128, 8192], BF16))
        self.pb = [sb(nc.psum_tensor(f"pb{i}", [128, 512], F32)) for i in range(6)]
        self.pv = sb(nc.psum_tensor("pv", [128, 512], F32))
        self.ptr = sb(nc.psum_tensor("ptr", [128, 1024], BF16))

    def sem(self, name):
        if name not in self._sems:
            self._sems[name] = self.stack.enter_context(self.nc.semaphore(name))
            self.c[name] = 0
        return self._sems[name]

    def engines(self):
        nc = self.nc
        return [("te", nc.tensor), ("v", nc.vector), ("s", nc.scalar),
                ("g", nc.gpsimd), ("sp", nc.sync)]

    def barrier(self):
        self.nc.all_engine_barrier()

    WAITERS = {"at": ["te", "v", "s", "g"], "aldk": ["te", "v", "g"], "ast": ["v", "g", "te"],
               "ldkv": ["v", "sp", "te", "g", "s"], "ph": ["g", "v", "te", "sp"], "ph2": ["s", "g"],
               "mldx": ["te", "sp", "g"], "rt": ["v", "s", "te", "g"],
               "ldw": ["te", "g", "v", "s"],
               "pe": ["te", "v", "s", "g", "sp"], "fin": ["v", "sp", "g"], "ft": ["v", "g"],
               "ldx": ["te", "v", "s", "g", "sp"], "st": ["v", "g", "te", "s", "sp"],
               "ldr": ["v", "sp", "g", "te"], "lnm": ["s", "v", "g", "sp"],
               "ldc": ["v", "te", "g", "s", "sp"], "ldk": ["te", "v", "g"]}

    def lc_begin(self, budgets):
        for name, per in budgets.items():
            self.sem(name)
            assert name not in self.lc, name
            self.lc[name] = {"per": per, "pybase": self.c[name], "bodies": 0}
            for en, eng in self.engines():
                if en not in self.WAITERS[name]:
                    continue
                key = (en, name)
                if key not in self.regs:
                    self.regs[key] = (eng.alloc_register(f"b_{en}_{name}"),
                                      eng.alloc_register(f"t_{en}_{name}"))
                eng.reg_mov(self.regs[key][0], self.c[name])
            self.c[name] = 0

    def body_end(self, names):
        for name in names:
            info = self.lc[name]
            used = self.c[name]
            assert used <= info["per"], f"{name}: used {used} > budget {info['per']}"
            if used < info["per"]:
                self.wait(self.nc.gpsimd, name)
                self.nc.gpsimd.sem_inc(self._sems[name], info["per"] - used)
            for en, eng in self.engines():
                if en not in self.WAITERS[name]:
                    continue
                b, _ = self.regs[(en, name)]
                eng.reg_add(b, b, info["per"])
            self.c[name] = 0
            info["bodies"] += 1

    def lc_end(self, nbodies):
        for name, nb in nbodies.items():
            info = self.lc.pop(name)
            self.c[name] = info["pybase"] + nb * info["per"]

    def dma(self, q, out, in_, semname, transpose=False):
        s = self.sem(semname)
        if transpose:
            q.dma_start(out=out, in_=in_, transpose=True).then_inc(s, 16)
        else:
            q.dma_start(out=out, in_=in_).then_inc(s, 16)
        self.c[semname] += 16
        return self.c[semname]

    def dma_i(self, q, out, in_flat, idx_ap, semname):
        from concourse.bass import IndirectOffsetOnAxis
        s = self.sem(semname)
        q.indirect_dma_start(out, None, in_flat,
                             IndirectOffsetOnAxis(idx_ap, 0)).then_inc(s, 16)
        self.c[semname] += 16
        return self.c[semname]

    def wait(self, eng, semname, n=None):
        tgt = self.c[semname] if n is None else n
        if semname in self.lc:
            en = {id(e): nm for nm, e in self.engines()}[id(eng)]
            assert en in self.WAITERS[semname], (en, semname)
            b, t = self.regs[(en, semname)]
            if tgt == 0:
                eng.wait_ge(self._sems[semname], b)
            else:
                eng.reg_add(t, b, tgt)
                eng.wait_ge(self._sems[semname], t)
        else:
            eng.wait_ge(self._sems[semname], tgt)

    def tick(self, inst, semname="at"):
        s = self.sem(semname)
        self.c[semname] += 1
        inst.then_inc(s, 1)
        return self.c[semname]



def host_rope():
    hd = 64
    inv = 1.0 / (10000.0 ** (np.arange(0, hd, 2) / hd))
    t = np.arange(T, dtype=np.float64)
    fr = t[:, None] * inv[None, :]
    c = np.cos(fr).T
    s = np.sin(fr).T
    R = np.zeros((64, 2 * T), np.float32)
    R[0:32, 0:T] = c; R[32:64, 0:T] = c
    R[0:32, T:2 * T] = s; R[32:64, T:2 * T] = s
    return R


def host_rope_scaled():
    R = host_rope()
    out = np.zeros((64, 16384), np.float32)
    out[:, 0:8192] = R / 8.0
    out[:, 8192:16384] = R
    return out


def host_xpos():
    hd, W2 = 64, 512
    half = np.arange(0, hd, 2, dtype=np.float64)
    inv = 1.0 / (10000.0 ** (half / hd))
    t = np.arange(T, dtype=np.float64)
    fr = t[:, None] * inv[None, :]
    base = (half + 0.4 * hd) / (1.4 * hd)
    power = (t - T // 2) / float(W2 // 2)
    scale = base[None, :] ** power[:, None]
    c, s = np.cos(fr), np.sin(fr)
    X = np.zeros((64, 4 * T), np.float32)
    Aq = (c * scale * 8.0).T; Bq = (s * scale * 8.0).T
    Ak = (c / scale).T; Bk = (s / scale).T
    X[0:32, 0:T] = Aq; X[32:64, 0:T] = Aq
    X[0:32, T:2 * T] = Bq; X[32:64, T:2 * T] = Bq
    X[0:32, 2 * T:3 * T] = Ak; X[32:64, 2 * T:3 * T] = Ak
    X[0:32, 3 * T:4 * T] = Bk; X[32:64, 3 * T:4 * T] = Bk
    return X


def host_consts():
    C = np.zeros((128, 8464 + 64 + 1024), dtype=np.float32)
    i_idx = np.arange(128)[:, None]
    j_idx = np.arange(512)[None, :]
    for s4 in range(4):
        C[:, s4 * 512:(s4 + 1) * 512] = np.where(j_idx <= 128 * s4 + i_idx, 0.0, NEG)
    for p in range(4):
        C[:, (4 + p) * 512:(5 + p) * 512] = np.where(128 * p + i_idx <= j_idx, 0.0, NEG)
    for s4 in range(4):
        C[:, (8 + s4) * 512:(9 + s4) * 512] = np.where(j_idx >= 128 * s4 + i_idx, 0.0, NEG)
    for p in range(4):
        C[:, (12 + p) * 512:(13 + p) * 512] = np.where(128 * p + i_idx >= j_idx, 0.0, NEG)
    C[:, 8192:8320] = np.eye(128, dtype=np.float32)
    C[:, 8320:8448] = 1.0
    C[:, 8448] = 1e-5
    C[:, 8449] = 1e-12
    P = np.zeros((64, 64), np.float32)
    for m in range(32):
        P[m + 32, m] = -1.0
        P[m, m + 32] = 1.0
    C[0:64, 8464:8528] = P
    for e in range(8):
        C[e, 8528 + e * 128: 8528 + (e + 1) * 128] = 1.0
    return C


def load_consts(ctx, CONSTS):
    nc = ctx.nc
    ctx.dma(nc.gpsimd, ctx.maskb[:], CONSTS[:, 0:8192], "ldc")
    ctx.dma(nc.gpsimd, ctx.ident[:], CONSTS[:, 8192:8320], "ldc")
    ctx.dma(nc.gpsimd, ctx.onesrow[:], CONSTS[:, 8320:8448], "ldc")
    ctx.dma(nc.gpsimd, ctx.onescol[:], CONSTS[:, 8320:8336], "ldc")
    ctx.dma(nc.sync, ctx.cf32[:], CONSTS[:, 8448:8464], "ldc")
    ctx.dma(nc.gpsimd, ctx.prot[:], CONSTS[0:64, 8464:8528], "ldc")
    ctx.dma(nc.gpsimd, ctx.sel[:], CONSTS[0:8, 8528:9552], "ldc")
    ctx.wait(nc.vector, "ldc")
    ctx.barrier()


def load_weights_bf(ctx, W_ap, d_in, d_out, col0=0):
    for c in range(d_in // 128):
        ctx.dma(ctx.nc.gpsimd, ctx.wbuf[c][:, col0:col0 + d_out],
                W_ap[c * 128:(c + 1) * 128, :], "ldw")


def load_src_fm(ctx, src_tm_ap, chunks=None):
    for c in (chunks if chunks is not None else range(8)):
        ctx.dma(ctx.nc.sync, ctx.srcfm[c][:],
                src_tm_ap[:, c * 128:(c + 1) * 128], "ldx", transpose=True)



def emit_ln_tm(ctx, src_ap, dst_bf_ap, eps=1e-5, dst_f32_ap=None, rms=False,
               w_ap=None, b_ap=None, shift=False):
    nc = ctx.nc
    v, s = nc.vector, nc.scalar
    x0, y0, rows, b0 = ctx.x0, ctx.y0, ctx.rows, ctx.b0
    wbc = bbc = None
    if w_ap is not None:
        wbc = ctx.y1
        ctx.dma(nc.sync, wbc[:], w_ap.rearrange("(a d) -> a d", a=1).broadcast_to((128, D)), "ldx")
        if b_ap is not None:
            bbc = ctx.x1
            ctx.dma(nc.sync, bbc[:], b_ap.rearrange("(a d) -> a d", a=1).broadcast_to((128, D)), "ldx")
    ctx.sem("lnm")
    for t in range(T // 128):
        r = t * 128
        if t > 0:
            ctx.wait(nc.sync, "lnm")
        if not shift:
            n = ctx.dma(nc.sync, x0[:], src_ap[r:r + 128, :], "ldx")
        else:
            n = ctx.dma(nc.sync, x0[:, 0:512], src_ap[r:r + 128, 0:512], "ldx")
            if t == 0:
                n = ctx.dma(nc.sync, x0[1:128, 512:1024], src_ap[0:127, 512:1024], "ldx")
            else:
                n = ctx.dma(nc.sync, x0[:, 512:1024], src_ap[r - 1:r + 127, 512:1024], "ldx")
        ctx.wait(v, "ldx", n)
        if shift and t == 0:
            v.memset(x0[0:1, 512:1024], 0.0)
        lnm = ctx.sem("lnm")
        if rms:
            v.tensor_mul(y0[:], x0[:], x0[:])
            i1 = v.reduce_sum(rows[:, 4:5], y0[:], axis=AX.X)
            ctx.tick(i1, "lnm"); ctx.wait(v, "lnm")
            i2 = v.tensor_scalar_mul(rows[:, 28:29], rows[:, 4:5], 1.0 / D)
            ctx.tick(i2, "lnm")
        else:
            v.reduce_sum(rows[:, 0:1], x0[:], axis=AX.X)
            v.tensor_mul(y0[:], x0[:], x0[:])
            i1 = v.reduce_sum(rows[:, 4:5], y0[:], axis=AX.X)
            ctx.tick(i1, "lnm"); ctx.wait(v, "lnm")
            v.tensor_scalar_mul(rows[:, 16:17], rows[:, 0:1], 1.0 / D)
            v.tensor_scalar_mul(rows[:, 20:21], rows[:, 4:5], 1.0 / D)
            i2 = v.tensor_mul(rows[:, 24:25], rows[:, 16:17], rows[:, 16:17])
            ctx.tick(i2, "lnm"); ctx.wait(v, "lnm")
            i3 = v.tensor_sub(rows[:, 28:29], rows[:, 20:21], rows[:, 24:25])
            ctx.tick(i3, "lnm")
        ctx.wait(s, "lnm")
        s.activation(rows[:, 32:33], rows[:, 28:29], AF.Ln, bias=ctx.cf32[:, 0:1])
        i4 = s.activation(rows[:, 36:37], rows[:, 32:33], AF.Exp, scale=-0.5)
        ctx.tick(i4, "lnm")
        ctx.wait(v, "lnm")
        if rms:
            v.scalar_tensor_tensor(y0[:], x0[:], rows[:, 36:37], wbc[:], ALU.mult, ALU.mult)
        else:
            v.tensor_scalar(y0[:], x0[:], rows[:, 16:17], rows[:, 36:37], ALU.subtract, ALU.mult)
            if w_ap is not None:
                iv = v.tensor_mul(y0[:], y0[:], wbc[:])
                ctx.tick(iv, "lnm"); ctx.wait(v, "lnm")
                if bbc is not None:
                    iv = v.tensor_add(y0[:], y0[:], bbc[:])
                    ctx.tick(iv, "lnm"); ctx.wait(v, "lnm")
        i5 = v.tensor_copy(b0[:, 0:1024], y0[:])
        ctx.tick(i5, "lnm")
        ctx.wait(nc.gpsimd, "lnm")
        nst = ctx.dma(nc.gpsimd, dst_bf_ap[r:r + 128, :], b0[:, 0:1024], "st")
        if dst_f32_ap is not None:
            nst = ctx.dma(nc.gpsimd, dst_f32_ap[r:r + 128, :], y0[:], "st")
        ctx.wait(v, "st", nst)
    ctx.barrier()



def emit_proj_fm(ctx, dst_fm_ap, d_out, wcol0=0, out_scale=1.0, nch=8):
    nc = ctx.nc
    pe = ctx.sem("pe")
    ctx.wait(nc.tensor, "ldx")
    ctx.wait(nc.tensor, "ldw")
    for j in range(d_out // 128):
        for tb in range(T // 512):
            g = j * (T // 512) + tb
            bank = ctx.pb[g % 2]
            if g >= 2:
                ctx.wait(nc.tensor, "pe")
            last = None
            for c in range(nch):
                last = nc.tensor.matmul(bank[:], ctx.wbuf[c][:, wcol0 + j * 128:wcol0 + (j + 1) * 128],
                                        ctx.srcfm[c][:, tb * 512:(tb + 1) * 512],
                                        start=(c == 0), stop=(c == nch - 1))
            mm_done = ctx.tick(last, "pe")
            ctx.wait(nc.scalar, "pe", mm_done)
            iact = nc.scalar.activation(ctx.b0[:, 0:512], bank[:], AF.Copy, scale=float(out_scale))
            ev_done = ctx.tick(iact, "pe")
            ctx.wait(nc.gpsimd, "pe", ev_done)
            n = ctx.dma(nc.gpsimd, dst_fm_ap[j * 128:(j + 1) * 128, tb * 512:(tb + 1) * 512],
                        ctx.b0[:, 0:512], "st")
            ctx.wait(nc.scalar, "st", n)
    ctx.barrier()


def emit_proj_tm(ctx, dst_tm_ap, d_out, wcol0=0, aug65=False, nch=8):
    nc = ctx.nc
    pe = ctx.sem("pe")
    ctx.wait(nc.tensor, "ldx")
    ctx.wait(nc.tensor, "ldw")
    ncolg = (d_out + 511) // 512
    for t in range(T // 128):
        for cg in range(ncolg):
            gidx = t * ncolg + cg
            cols = min(512, d_out - cg * 512)
            bank = ctx.pb[gidx % 2]
            if gidx >= 2:
                ctx.wait(nc.tensor, "pe")
            last = None
            for c in range(nch):
                last = nc.tensor.matmul(bank[:, 0:cols],
                                        ctx.srcfm[c][:, t * 128:(t + 1) * 128],
                                        ctx.wbuf[c][:, wcol0 + cg * 512: wcol0 + cg * 512 + cols],
                                        start=(c == 0), stop=(c == nch - 1))
            mm_done = ctx.tick(last, "pe")
            ctx.wait(nc.scalar, "pe", mm_done)
            if not aug65:
                iact = nc.scalar.activation(ctx.b0[:, 0:cols], bank[:, 0:cols], AF.Copy)
                ev = ctx.tick(iact, "pe")
                ctx.wait(nc.gpsimd, "pe", ev)
                n = ctx.dma(nc.gpsimd, dst_tm_ap[t * 128:(t + 1) * 128, cg * 512:cg * 512 + cols],
                            ctx.b0[:, 0:cols], "st")
                ctx.wait(nc.scalar, "st", n)
            else:
                hpg = cols // 64
                dst0 = (cg * 512 // 64) * 65
                ia = nc.scalar.activation(
                    ctx.b1[:, 0:hpg * 65].rearrange("p (h e) -> p h e", e=65)[:, :, 0:64],
                    bank[:, 0:cols].rearrange("p (h e) -> p h e", e=64), AF.Copy)
                ctx.wait(nc.vector, "pe", mm_done)
                im = nc.vector.memset(
                    ctx.b1[:, 0:hpg * 65].rearrange("p (h e) -> p h e", e=65)[:, :, 64:65], 1.0)
                ctx.tick(ia, "pe")
                ev = ctx.tick(im, "pe")
                ctx.wait(nc.gpsimd, "pe", ev)
                n = ctx.dma(nc.gpsimd, dst_tm_ap[t * 128:(t + 1) * 128, dst0:dst0 + hpg * 65],
                            ctx.b1[:, 0:hpg * 65], "st")
                ctx.wait(nc.scalar, "st", n)
                ctx.wait(nc.vector, "st", n)
    ctx.barrier()


def emit_proj_fm_rope(ctx, dst_fm_ap, nheads, wcol0=0):
    nc = ctx.nc
    v, s, te = nc.vector, nc.scalar, nc.tensor
    pe = ctx.sem("pe")
    ctx.wait(te, "ldx")
    ctx.wait(te, "ldw")
    ctx.wait(te, "ldc")
    for h in range(nheads):
        for tb in range(T // 512):
            g = h * (T // 512) + tb
            bank = ctx.pb[g % 2]
            rbank = ctx.pb[2 + g % 2]
            if g >= 2:
                ctx.wait(te, "pe")
            last = None
            for c in range(8):
                last = te.matmul(bank[0:64, :], ctx.wbuf[c][:, wcol0 + h * 64:wcol0 + h * 64 + 64],
                                 ctx.srcfm[c][:, tb * 512:(tb + 1) * 512],
                                 start=(c == 0), stop=(c == 7))
            mm = ctx.tick(last, "pe")
            ctx.wait(s, "pe", mm)
            ie = s.activation(ctx.b1[0:64, 0:512], bank[0:64, :], AF.Copy)
            ev1 = ctx.tick(ie, "pe")
            ctx.wait(te, "pe", ev1)
            irt = te.matmul(rbank[0:64, :], ctx.prot[:], ctx.b1[0:64, 0:512],
                            start=True, stop=True)
            rt = ctx.tick(irt, "pe")
            ctx.wait(v, "pe", rt)
            tc = slice(tb * 512, (tb + 1) * 512)
            v.tensor_mul(ctx.rows2[0:64, :], bank[0:64, :], ctx.maskb[0:64, tc])
            iv2 = v.tensor_mul(ctx.y1[0:64, 0:512], rbank[0:64, :],
                               ctx.maskb[0:64, T + tb * 512:T + (tb + 1) * 512])
            ctx.tick(iv2, "pe"); ctx.wait(v, "pe")
            iv3 = v.tensor_add(ctx.b0[0:64, 0:512], ctx.rows2[0:64, :], ctx.y1[0:64, 0:512])
            ev = ctx.tick(iv3, "pe")
            ctx.wait(nc.gpsimd, "pe", ev)
            n = ctx.dma(nc.gpsimd, dst_fm_ap[h * 64:h * 64 + 64, tc], ctx.b0[0:64, 0:512], "st")
            ctx.wait(v, "st", n)
            ctx.wait(s, "st", n)
    ctx.barrier()



def _attn_load_kv(ctx, kvh_expr, KF, VT, QF, QCUR, kv_group):
    nc = ctx.nc
    v = nc.vector
    KFh = KF.rearrange("(h d) t -> h d t", d=64)
    VTh = VT.rearrange("(j p) (h c) -> h p j c", p=128, c=65)
    QFg = QF.rearrange("(h g) t -> h g t", g=64 * kv_group)
    ctx.barrier()
    ctx.dma(nc.sync, ctx.k_sb[0:64, :], KFh[kvh_expr], "ldkv")
    ctx.dma(nc.sync, ctx.v_sb[:, 0:2080].rearrange("p (j c) -> p j c", c=65),
            VTh[kvh_expr], "ldkv")
    ctx.dma(nc.sync, QCUR[0:64 * kv_group, :], QFg[kvh_expr], "ldkv")
    ctx.wait(v, "ldkv")
    v.memset(ctx.k_sb[64:65, :], 1.0)
    ctx.barrier()


def emit_attention(ctx, QF, KF, VT, OF, nheads=16, kv_group=1, mode="full",
                   hooks_factory=None, per_head=None, per_head_budget=0,
                   QCUR=None, OCUR=None, manage_lc=True):
    nc = ctx.nc
    NQ = T // 512
    nkv = nheads // kv_group
    ATB = 2048
    QFb = QCUR.rearrange("(h d) (b q) -> h b d q", d=64, q=512)
    OFb = OCUR.rearrange("(h d) (b q) -> h b d q", d=64, q=512)
    OFg = OF.rearrange("(h g) t -> h g t", g=64 * kv_group)
    budgets = {"at": ATB, "aldk": 16, "ast": 16, "ldkv": 16 * 5}
    if per_head is not None:
        budgets["ph"] = 16 * 16
        budgets["ph2"] = 8
    if manage_lc:
        ctx.lc_begin(budgets)
    with nc.Fori(0, nkv) as kvh:
        _attn_load_kv(ctx, kvh, KF, VT, QF, QCUR, kv_group)
        if per_head is not None:
            per_head(ctx, kvh)
            ctx.barrier()
        for hg in range(kv_group):
            h_expr = hg
            if mode == "full" and hooks_factory is None:
                with nc.Fori(0, NQ) as qbv:
                    _attn_qblock_body(ctx, h_expr, qbv, QFb, OFb, list(range(NQ)), None)
                    ctx.body_end(["at", "aldk", "ast"])
            else:
                for qb in range(NQ):
                    live = (list(range(qb + 1)) if mode == "causal" else
                            list(range(qb, NQ)) if mode == "anticausal" else
                            [t2 for t2 in (qb - 1, qb) if t2 >= 0] if mode == "window" else
                            list(range(NQ)))
                    if hooks_factory is not None:
                        masksel, abias, bbias = hooks_factory(qb)
                    elif mode == "window":
                        def masksel(kt, i, phase, _qb=qb):
                            if kt == _qb:
                                return i if phase == "A" else 4 + i
                            return (8 + i) if phase == "A" else (12 + i)
                        abias = bbias = None
                    else:
                        maskA0 = 0 if mode != "anticausal" else 8
                        maskB0 = 4 if mode != "anticausal" else 12
                        def masksel(kt, i, phase, _qb=qb, _a=maskA0, _b=maskB0):
                            if kt != _qb:
                                return None
                            return (_a + i) if phase == "A" else (_b + i)
                        abias = bbias = None
                    _attn_qblock_body(ctx, h_expr, qb, QFb, OFb, live, masksel,
                                      abias=abias, bbias=bbias)
                    ctx.body_end(["at", "aldk", "ast"])
        ctx.barrier()
        ctx.dma(nc.sync, OFg[kvh], OCUR[0:64 * kv_group, :], "ldkv")
        ctx.wait(nc.sync, "ldkv")
        ctx.body_end(["ldkv"] + (["ph", "ph2"] if per_head is not None else []))
    nqb = nkv * kv_group * NQ
    if manage_lc:
        ends = {"at": nqb, "aldk": nqb, "ast": nqb, "ldkv": nkv}
        if per_head is not None:
            ends["ph"] = nkv
            ends["ph2"] = nkv
        ctx.lc_end(ends)
    ctx.barrier()


def _attn_qblock_body(ctx, h_expr, qb_expr, QFb, OFb, live_kt, masksel, phases='ABC', abias=None, bbias=None, qscale=None, skip_qload=False, skip_ostore=False):
    nc = ctx.nc
    v, s, te, g = nc.vector, nc.scalar, nc.tensor, nc.gpsimd
    ats = ctx.sem("at")
    ctx.barrier()
    if not skip_qload:
        nq = ctx.dma(nc.sync, ctx.q_sb[0:64, 0:512], QFb[h_expr, qb_expr], "aldk")
        ctx.wait(te, "aldk", nq)
    for sub in range(4):
        q0 = sub * 128
        for ki, kt in enumerate(live_kt):
            bank = ctx.pb[ki % 2]
            imm = te.matmul(bank[:], ctx.q_sb[0:64, q0:q0 + 128],
                            ctx.k_sb[0:64, kt * 512:(kt + 1) * 512], start=True, stop=True)
            mm = ctx.tick(imm, "at")
            ctx.wait(v, "at", mm)
            msl = masksel(kt, sub, "A") if masksel else None
            if msl is not None:
                iv = v.scalar_tensor_tensor(ctx.rows2[:], bank[:], 1.0,
                                            ctx.maskb[:, msl * 512:(msl + 1) * 512],
                                            ALU.mult, ALU.add)
                ctx.tick(iv, "at"); ctx.wait(v, "at")
                ird = v.reduce_max(ctx.rows[:, 16 + ki:17 + ki], ctx.rows2[:], axis=AX.X)
            else:
                ird = v.reduce_max(ctx.rows[:, 16 + ki:17 + ki], bank[:], axis=AX.X)
            red = ctx.tick(ird, "at")
            if abias is not None and abias(kt, sub) is not None:
                ctx.wait(v, "at")
                iab = v.tensor_scalar_add(ctx.rows[:, 16 + ki:17 + ki],
                                          ctx.rows[:, 16 + ki:17 + ki], abias(kt, sub))
                red = ctx.tick(iab, "at")
            ctx.wait(te, "at", red)
        ctx.wait(v, "at")
        irf = v.reduce_max(ctx.rows[:, 8:9], ctx.rows[:, 16:16 + len(live_kt)],
                           axis=AX.X, negate=True)
        ctx.tick(irf, "at"); ctx.wait(v, "at")
        icp = v.tensor_copy(ctx.rbf[:, 0:1], ctx.rows[:, 8:9])
        nm = ctx.tick(icp, "at")
        ctx.wait(te, "at", nm)
        itr = te.transpose(ctx.ptr[0:1, 0:128], ctx.rbf[:, 0:1], ctx.ident[:])
        tr = ctx.tick(itr, "at")
        ctx.wait(s, "at", tr)
        isa = s.activation(ctx.q_sb[64:65, q0:q0 + 128], ctx.ptr[0:1, 0:128], AF.Copy)
        ctx.tick(isa, "at")
    ctx.wait(te, "at")
    if phases == "A":
        return
    flat = [(kt, j) for kt in live_kt for j in range(4)]
    for fi, (kt, j) in enumerate(flat):
        k0 = kt * 512 + j * 128
        bank = ctx.pb[2 + (fi % 2)]
        imm = te.matmul(bank[:], ctx.k_sb[0:65, k0:k0 + 128], ctx.q_sb[0:65, 0:512],
                        start=True, stop=True)
        mm = ctx.tick(imm, "at")
        pslot = ctx.p_sb[:, (fi % 2) * 512:((fi % 2) + 1) * 512]
        msl = masksel(kt, j, "B") if masksel else None
        if msl is not None:
            ctx.wait(v, "at", mm)
            imk = v.scalar_tensor_tensor(ctx.rows2[:], bank[:], 1.0,
                                         ctx.maskb[:, msl * 512:(msl + 1) * 512],
                                         ALU.mult, ALU.add)
            mk = ctx.tick(imk, "at")
            ctx.wait(s, "at", mk)
            iex = s.activation(pslot, ctx.rows2[:], AF.Exp)
        else:
            ctx.wait(s, "at", mm)
            bb = bbias(kt, j) if bbias is not None else None
            if bb is not None:
                iex = s.activation(pslot, bank[:], AF.Exp, bias=bb)
            else:
                iex = s.activation(pslot, bank[:], AF.Exp)
        ex = ctx.tick(iex, "at")
        ctx.wait(te, "at", ex)
        ipv = te.matmul(ctx.pv[0:65, :], ctx.v_sb[:, (k0 // 128) * 65:(k0 // 128) * 65 + 65],
                        pslot, start=(fi == 0), stop=(fi == len(flat) - 1),
                        skip_group_check=True)
    pv_done = ctx.tick(ipv, "at")
    if phases == "AB":
        return
    ctx.wait(v, "at", pv_done)
    ir = v.reciprocal(ctx.rows2[64:65, 0:512], ctx.pv[64:65, 0:512])
    ctx.tick(ir, "at"); ctx.wait(v, "at")
    icp = v.tensor_copy(ctx.rbf[64:65, 0:512], ctx.rows2[64:65, 0:512])
    rc = ctx.tick(icp, "at")
    ctx.wait(te, "at", rc)
    ibc = te.matmul(ctx.pb[4][0:64, 0:512], ctx.onesrow[64:65, 0:64], ctx.rbf[64:65, 0:512],
                    start=True, stop=True)
    bc = ctx.tick(ibc, "at")
    ctx.wait(s, "at", bc)
    ibs = s.activation(ctx.b0[0:64, 0:512], ctx.pb[4][0:64, 0:512], AF.Copy)
    bcs = ctx.tick(ibs, "at")
    ctx.wait(v, "at", bcs)
    ino = v.scalar_tensor_tensor(ctx.b1[0:64, 0:512], ctx.pv[0:64, 0:512], 1.0,
                                 ctx.b0[0:64, 0:512], ALU.mult, ALU.mult)
    no = ctx.tick(ino, "at")
    if skip_ostore:
        return
    ctx.wait(g, "at", no)
    nst = ctx.dma(nc.gpsimd, OFb[h_expr, qb_expr], ctx.b1[0:64, 0:512], "ast")
    ctx.wait(v, "ast", nst)
    ctx.wait(g, "ast", nst)


def emit_out_proj_residual(ctx, OF, XR, d_in=1024):
    nc = ctx.nc
    pe = ctx.sem("pe")
    for c in (range(d_in // 128) if OF is not None else ()):
        ctx.dma(nc.sync, ctx.srcfm[c][:], OF[c * 128:(c + 1) * 128, :], "ldx")
    ctx.wait(nc.tensor, "ldx")
    ctx.wait(nc.tensor, "ldw")
    nch = d_in // 128
    for t in range(T // 128):
        if t > 0:
            ctx.wait(nc.sync, "pe")
        n = ctx.dma(nc.sync, ctx.x0[:], XR[t * 128:(t + 1) * 128, :], "ldr")
        if t >= 1:
            ctx.wait(nc.tensor, "pe")
        last = None
        for cg in range(2):
            bank = ctx.pb[cg]
            for c in range(nch):
                last = nc.tensor.matmul(bank[:],
                                        ctx.srcfm[c][:, t * 128:(t + 1) * 128],
                                        ctx.wbuf[c][:, cg * 512:(cg + 1) * 512],
                                        start=(c == 0), stop=(c == nch - 1))
        mm = ctx.tick(last, "pe")
        ctx.wait(nc.vector, "pe", mm)
        ctx.wait(nc.vector, "ldr", n)
        nc.vector.tensor_add(ctx.y0[:, 0:512], ctx.pb[0][:], ctx.x0[:, 0:512])
        iad = nc.vector.tensor_add(ctx.y0[:, 512:1024], ctx.pb[1][:], ctx.x0[:, 512:1024])
        ad = ctx.tick(iad, "pe")
        ctx.wait(nc.gpsimd, "pe", ad)
        nst = ctx.dma(nc.gpsimd, XR[t * 128:(t + 1) * 128, :], ctx.y0[:], "st")
        ctx.wait(nc.vector, "st", nst)
    ctx.barrier()


def emit_out_proj(ctx, OF, src_ap, dst_ap, d_in=1024, clip=None, extra_res_ap=None):
    nc = ctx.nc
    pe = ctx.sem("pe")
    for c in range(d_in // 128):
        ctx.dma(nc.sync, ctx.srcfm[c][:], OF[c * 128:(c + 1) * 128, :], "ldx")
    ctx.wait(nc.tensor, "ldx")
    ctx.wait(nc.tensor, "ldw")
    nch = d_in // 128
    for t in range(T // 128):
        if t > 0:
            ctx.wait(nc.sync, "pe")
        n = ctx.dma(nc.sync, ctx.x0[:], src_ap[t * 128:(t + 1) * 128, :], "ldr")
        if extra_res_ap is not None:
            n = ctx.dma(nc.sync, ctx.x1[:], extra_res_ap[t * 128:(t + 1) * 128, :], "ldr")
        if t >= 1:
            ctx.wait(nc.tensor, "pe")
        last = None
        for cg in range(2):
            bank = ctx.pb[cg]
            for c in range(nch):
                last = nc.tensor.matmul(bank[:],
                                        ctx.srcfm[c][:, t * 128:(t + 1) * 128],
                                        ctx.wbuf[c][:, cg * 512:(cg + 1) * 512],
                                        start=(c == 0), stop=(c == nch - 1))
        mm = ctx.tick(last, "pe")
        ctx.wait(nc.vector, "pe", mm)
        ctx.wait(nc.vector, "ldr", n)
        i1 = nc.vector.tensor_add(ctx.y0[:, 0:512], ctx.pb[0][:], ctx.x0[:, 0:512])
        i1 = nc.vector.tensor_add(ctx.y0[:, 512:1024], ctx.pb[1][:], ctx.x0[:, 512:1024])
        if clip is not None:
            ctx.tick(i1, "pe"); ctx.wait(nc.vector, "pe")
            i1 = nc.vector.tensor_scalar(ctx.y0[:], ctx.y0[:], clip, -clip, ALU.min, ALU.max)
        if extra_res_ap is not None:
            ctx.tick(i1, "pe"); ctx.wait(nc.vector, "pe")
            i1 = nc.vector.tensor_add(ctx.y0[:], ctx.y0[:], ctx.x1[:])
        ad = ctx.tick(i1, "pe")
        ctx.wait(nc.gpsimd, "pe", ad)
        nst = ctx.dma(nc.gpsimd, dst_ap[t * 128:(t + 1) * 128, :], ctx.y0[:], "st")
        ctx.wait(nc.vector, "st", nst)
    ctx.barrier()


def emit_moe(ctx, hn2_tm, h_ap, XR, Wr, Wg, Wu, Wd, scr, dbg_moe=None):
    nc = ctx.nc
    v, s, te, g = nc.vector, nc.scalar, nc.tensor, nc.gpsimd
    WrT = Wr.rearrange("(c p) e -> c p e", p=128)
    for c in range(8):
        ctx.dma(g, ctx.wr[:, c * 8:(c + 1) * 8], WrT[c], "ldw")
    ctx.wait(te, "ldw")
    ctx.barrier()
    Wg3 = Wg.rearrange("(e d) f -> e d f", e=8)
    Wu3 = Wu.rearrange("(e d) f -> e d f", e=8)
    Wd3 = Wd.rearrange("(e f) d -> e f d", e=8)
    ctx.lc_begin({"mldx": 16 * 12, "rt": 128, "ldw": 16 * 40, "pe": 192,
                  "fin": 16 * 32, "ft": 64})
    NB = T // 512
    HT = hn2_tm.rearrange("(b q) d -> b q d", q=512)
    Hb = h_ap.rearrange("(b q) d -> b q d", q=512)
    XRb = XR.rearrange("(b q) d -> b q d", q=512)
    HNC, HC, XC = scr["hncur"], scr["hcur"], scr["xcur"]
    with nc.Fori(0, NB) as tb:
        ctx.barrier()
        nc.vector.memset(ctx.accm[:], 0.0)
        ctx.dma(nc.sync, HNC, HT[tb], "mldx")
        ctx.dma(nc.sync, HC, Hb[tb], "mldx")
        ctx.dma(nc.sync, XC, XRb[tb], "mldx")
        ctx.wait(nc.sync, "mldx")
        for c in range(8):
            ctx.dma(nc.sync, ctx.srcfm[c][:, 0:512],
                    HNC[:, c * 128:(c + 1) * 128], "mldx", transpose=True)
        ctx.wait(te, "mldx")
        for sub in range(4):
            last = None
            for c in range(8):
                last = te.matmul(ctx.pb[4][:, 0:8], ctx.srcfm[c][:, sub * 128:sub * 128 + 128],
                                 ctx.wr[:, c * 8:(c + 1) * 8], start=(c == 0), stop=(c == 7))
            mm = ctx.tick(last, "rt")
            ctx.wait(v, "rt", mm)
            R = ctx.rows
            i1 = v.reduce_max(R[:, 16:17], ctx.pb[4][:, 0:8], axis=AX.X, negate=True)
            ctx.tick(i1, "rt"); ctx.wait(s, "rt")
            i2 = s.activation(ctx.rows2[:, 0:8], ctx.pb[4][:, 0:8], AF.Exp, bias=R[:, 16:17])
            ctx.tick(i2, "rt"); ctx.wait(v, "rt")
            i3 = v.reduce_sum(R[:, 20:21], ctx.rows2[:, 0:8], axis=AX.X)
            ctx.tick(i3, "rt"); ctx.wait(v, "rt")
            i4 = v.reciprocal(R[:, 24:25], R[:, 20:21])
            ctx.tick(i4, "rt"); ctx.wait(v, "rt")
            i5 = v.tensor_scalar_mul(ctx.rows2[:, 8:16], ctx.rows2[:, 0:8], R[:, 24:25])
            ctx.tick(i5, "rt"); ctx.wait(v, "rt")
            i6 = v.reduce_max(R[:, 28:29], ctx.rows2[:, 8:16], axis=AX.X)
            ctx.tick(i6, "rt"); ctx.wait(v, "rt")
            i7 = v.tensor_single_scalar(ctx.rows2[:, 16:24], ctx.rows2[:, 8:16], R[:, 28:29], ALU.is_equal)
            ctx.tick(i7, "rt"); ctx.wait(v, "rt")
            i8 = v.tensor_scalar(ctx.rows2[:, 24:32], ctx.rows2[:, 16:24], -1.0, 1.0, ALU.mult, ALU.add)
            ctx.tick(i8, "rt"); ctx.wait(v, "rt")
            i9 = v.tensor_mul(ctx.rows2[:, 32:40], ctx.rows2[:, 8:16], ctx.rows2[:, 24:32])
            ctx.tick(i9, "rt"); ctx.wait(v, "rt")
            i10 = v.reduce_max(R[:, 32:33], ctx.rows2[:, 32:40], axis=AX.X)
            ctx.tick(i10, "rt"); ctx.wait(v, "rt")
            i11 = v.tensor_single_scalar(ctx.rows2[:, 40:48], ctx.rows2[:, 32:40], R[:, 32:33], ALU.is_equal)
            ctx.tick(i11, "rt"); ctx.wait(v, "rt")
            i12 = v.tensor_add(R[:, 36:37], R[:, 28:29], R[:, 32:33])
            ctx.tick(i12, "rt"); ctx.wait(v, "rt")
            i13 = v.reciprocal(R[:, 40:41], R[:, 36:37])
            ctx.tick(i13, "rt"); ctx.wait(v, "rt")
            i14 = v.tensor_add(ctx.rows2[:, 48:56], ctx.rows2[:, 16:24], ctx.rows2[:, 40:48])
            ctx.tick(i14, "rt"); ctx.wait(v, "rt")
            i15 = v.scalar_tensor_tensor(ctx.rows2[:, 56:64], ctx.rows2[:, 8:16], R[:, 40:41],
                                         ctx.rows2[:, 48:56], ALU.mult, ALU.mult)
            ctx.tick(i15, "rt"); ctx.wait(v, "rt")
            i16 = v.tensor_copy(ctx.b0[:, 1024:1032], ctx.rows2[:, 56:64])
            ctx.tick(i16, "rt"); ctx.wait(te, "rt")
            i17 = te.transpose(ctx.ptr[0:8, 0:128], ctx.b0[:, 1024:1032], ctx.ident[:])
            ctx.tick(i17, "rt"); ctx.wait(s, "rt")
            i18 = s.activation(ctx.rbf[0:8, sub * 128:sub * 128 + 128], ctx.ptr[0:8, 0:128], AF.Copy)
            ctx.tick(i18, "rt")
        ctx.wait(te, "rt")
        for e in range(8):
            ctx.wait(g, "pe")
            for c in range(8):
                ctx.dma(g, ctx.wbuf[c][:, 0:2048], Wg3[e][c * 128:(c + 1) * 128, :], "ldw")
            ctx.wait(te, "ldw")
            ie = te.matmul(ctx.pb[5][:, 0:512], ctx.sel[:, e * 128:(e + 1) * 128],
                           ctx.rbf[0:8, 0:512], start=True, stop=True)
            wb = ctx.tick(ie, "pe")
            ctx.wait(s, "pe", wb)
            iw = s.activation(ctx.wbc[:], ctx.pb[5][:, 0:512], AF.Copy)
            ctx.tick(iw, "pe")
            for fi in range(16):
                bank = ctx.pb[fi % 2]
                ctx.wait(te, "pe")
                last = None
                for c in range(8):
                    last = te.matmul(bank[:], ctx.wbuf[c][:, fi * 128:(fi + 1) * 128],
                                     ctx.srcfm[c][:, 0:512], start=(c == 0), stop=(c == 7))
                mm = ctx.tick(last, "pe")
                ctx.wait(s, "pe", mm)
                isl = s.activation(ctx.srcfm[fi % 8][:, 512 + (fi // 8) * 512:1024 + (fi // 8) * 512],
                                   bank[:], AF.Silu)
                ctx.tick(isl, "pe")
            ctx.wait(g, "pe")
            for c in range(8):
                ctx.dma(g, ctx.wbuf[c][:, 0:2048], Wu3[e][c * 128:(c + 1) * 128, :], "ldw")
            ctx.wait(te, "ldw")
            for fi in range(16):
                bank = ctx.pb[fi % 2]
                ctx.wait(te, "pe")
                last = None
                for c in range(8):
                    last = te.matmul(bank[:], ctx.wbuf[c][:, fi * 128:(fi + 1) * 128],
                                     ctx.srcfm[c][:, 0:512], start=(c == 0), stop=(c == 7))
                mm = ctx.tick(last, "pe")
                ctx.wait(v, "pe", mm)
                sg = ctx.srcfm[fi % 8][:, 512 + (fi // 8) * 512:1024 + (fi // 8) * 512]
                ap = ctx.srcfm[fi % 8][:, 1536 + (fi // 8) * 512:2048 + (fi // 8) * 512]
                ia = v.scalar_tensor_tensor(ap, bank[:], 1.0, sg, ALU.mult, ALU.mult)
                ctx.tick(ia, "pe"); ctx.wait(v, "pe")
                ia2 = v.tensor_mul(ap, ap, ctx.wbc[:])
                ctx.tick(ia2, "pe")
            ctx.wait(g, "pe")
            for c in range(16):
                ctx.dma(g, ctx.wbuf[c % 8][:, 1024 * (c // 8):1024 * (c // 8) + 1024],
                        Wd3[e][c * 128:(c + 1) * 128, :], "ldw")
            ctx.wait(te, "ldw")
            for sub in range(4):
                for half in range(2):
                    gidx = sub * 2 + half
                    bank = ctx.pb[2 + gidx % 2]
                    ctx.wait(te, "pe")
                    last = None
                    for c in range(16):
                        ap = ctx.srcfm[c % 8][:, 1536 + (c // 8) * 512:2048 + (c // 8) * 512]
                        last = te.matmul(bank[:],
                                         ap[:, sub * 128:sub * 128 + 128],
                                         ctx.wbuf[c % 8][:, 1024 * (c // 8) + half * 512:
                                                         1024 * (c // 8) + half * 512 + 512],
                                         start=(c == 0), stop=(c == 15))
                    mm = ctx.tick(last, "pe")
                    ctx.wait(v, "pe", mm)
                    dst = ctx.accm[:, sub * 1024 + half * 512: sub * 1024 + half * 512 + 512]
                    iv = v.tensor_add(dst, dst, bank[:])
                    ctx.tick(iv, "pe")
            if dbg_moe is not None:
                ctx.wait(g, "pe")
                ctx.dma(g, dbg_moe[2][e * 128:(e + 1) * 128, :], ctx.accm[:], "fin")
                ctx.wait(g, "fin")
            ctx.body_end(["ldw", "pe"])
        for sub in range(4):
            n = ctx.dma(nc.sync, ctx.x0[:], HC[sub * 128:sub * 128 + 128, :], "fin")
            n = ctx.dma(nc.sync, ctx.x1[:], XC[sub * 128:sub * 128 + 128, :], "fin")
            ctx.wait(v, "fin", n)
            ctx.wait(v, "pe")
            i1 = v.tensor_add(ctx.y0[:], ctx.accm[:, sub * 1024:(sub + 1) * 1024], ctx.x0[:])
            ctx.tick(i1, "ft"); ctx.wait(v, "ft")
            i2 = v.tensor_scalar(ctx.y0[:], ctx.y0[:], 100.0, -100.0, ALU.min, ALU.max)
            ctx.tick(i2, "ft"); ctx.wait(v, "ft")
            i3 = v.tensor_add(ctx.y0[:], ctx.y0[:], ctx.x1[:])
            fd = ctx.tick(i3, "ft")
            ctx.wait(g, "ft", fd)
            nst = ctx.dma(g, XC[sub * 128:sub * 128 + 128, :], ctx.y0[:], "fin")
            ctx.wait(v, "fin", nst)
            ctx.wait(nc.sync, "fin", nst)
        if dbg_moe is not None:
            ctx.dma(nc.sync, dbg_moe[0], ctx.rbf[0:8, 0:512], "fin")
            ctx.dma(nc.sync, dbg_moe[1], ctx.accm[:], "fin")
            ctx.wait(nc.sync, "fin")
        ctx.dma(nc.sync, XRb[tb], XC, "fin")
        ctx.wait(nc.sync, "fin")
        ctx.body_end(["mldx", "rt", "fin", "ft"])
    ctx.lc_end({"mldx": NB, "rt": NB, "fin": NB, "ft": NB, "ldw": NB * 8, "pe": NB * 8})
    ctx.barrier()


def emit_L0_ffn(ctx, XR, ln1_w, ln2_w, Wq, Wk, Wv, Wo, Wr, Wg, Wu, Wd, ROPE, CONSTS, scr):
    nc = ctx.nc
    emit_ln_tm(ctx, XR, scr["ln"], dst_f32_ap=scr["h0"])
    emit_ln_tm(ctx, scr["h0"], scr["ln2"], rms=True, w_ap=ln1_w)
    load_src_fm(ctx, scr["ln2"])
    ctx.dma(nc.gpsimd, ctx.maskb[0:64, 0:8192], ROPE[:, 0:8192], "ldc")
    load_weights_bf(ctx, Wq, D, 1024)
    emit_proj_fm_rope(ctx, scr["qf"], 16)
    ctx.dma(nc.gpsimd, ctx.maskb[0:64, 0:8192], ROPE[:, 8192:16384], "ldc")
    load_weights_bf(ctx, Wk, D, 256)
    emit_proj_fm_rope(ctx, scr["kf"], 4)
    load_weights_bf(ctx, Wv, D, 256)
    emit_proj_tm(ctx, scr["vt"], 256, aug65=True)
    emit_mask_load(ctx, CONSTS, "causal")
    emit_attention(ctx, scr["qf"], scr["kf"], scr["vt"], scr["of"],
                   nheads=16, kv_group=4, mode="causal",
                   QCUR=scr["qcur"], OCUR=scr["ocur"])
    load_weights_bf(ctx, Wo, D, 1024)
    emit_out_proj(ctx, scr["of"], scr["h0"], scr["h"], clip=100.0)
    emit_ln_tm(ctx, scr["h"], scr["ln2"], rms=True, w_ap=ln2_w)
    emit_moe(ctx, scr["ln2"], scr["h"], XR, Wr, Wg, Wu, Wd, scr,
             dbg_moe=getattr(build_program, "_dbg_moe", None))



def emit_L0_attn_b(ctx, XR, W_qkv, W_out, scr):
    emit_ln_tm(ctx, XR, scr["ln"])
    load_src_fm(ctx, scr["ln"])
    load_weights_bf(ctx, W_qkv[:, 0:1024], D, 1024)
    emit_proj_fm(ctx, scr["qf"], 1024, wcol0=0, out_scale=1.0 / 32.0)
    load_weights_bf(ctx, W_qkv[:, 1024:2048], D, 1024)
    emit_proj_fm(ctx, scr["kf"], 1024, wcol0=0)
    load_weights_bf(ctx, W_qkv[:, 2048:3072], D, 1024)
    emit_proj_tm(ctx, scr["vt"], 1024, wcol0=0, aug65=True)
    emit_attention(ctx, scr["qf"], scr["kf"], scr["vt"], scr["of"],
                   nheads=16, kv_group=1, mode="full",
                   QCUR=scr["qcur"], OCUR=scr["ocur"])
    load_weights_bf(ctx, W_out, D, 1024)
    emit_out_proj_residual(ctx, scr["of"], XR)


def emit_L1_attn_b(ctx, XR, Wq, Wk, Wv, Wo, T5OH, T5IDX, RB, scr):
    nc = ctx.nc
    v, s, te, g = nc.vector, nc.scalar, nc.tensor, nc.gpsimd
    emit_ln_tm(ctx, XR, scr["ln"])
    n1 = ctx.dma(g, ctx.rbf[0:32, 0:16], RB, "ldw")
    pe = ctx.sem("pe")
    for half in (0, 1):
        n1 = ctx.dma(g, ctx.k_sb[0:32, 0:4096], T5OH[0:32, half * 8191:half * 8191 + 4096], "ldw")
        n1 = ctx.dma(g, ctx.q_sb[0:32, 0:4095], T5OH[0:32, half * 8191 + 4096:half * 8191 + 8191], "ldw")
        ctx.wait(te, "ldw", n1)
        for chunk in range(16):
            cols = 512 if chunk < 15 else 511
            if chunk < 8:
                rhs = ctx.k_sb[0:32, chunk * 512:chunk * 512 + cols]
            else:
                rhs = ctx.q_sb[0:32, (chunk - 8) * 512:(chunk - 8) * 512 + cols]
            imm = te.matmul(ctx.pb[5][0:16, 0:cols], ctx.rbf[0:32, 0:16],
                            rhs, start=True, stop=True)
            ctx.tick(imm, "pe")
            ctx.wait(s, "pe")
            ie = s.activation(ctx.b0[0:16, 0:cols], ctx.pb[5][0:16, 0:cols], AF.Copy)
            ctx.tick(ie, "pe")
            ctx.wait(g, "pe")
            nst = ctx.dma(g, scr["t5diag"][half * 16:half * 16 + 16,
                                           chunk * 512:chunk * 512 + cols],
                          ctx.b0[0:16, 0:cols], "st")
            ctx.wait(s, "st", nst)
            ctx.wait(te, "pe")
        ctx.barrier()
    nix = ctx.dma(nc.sync, ctx.t5idx[:, 0:12], T5IDX, "ldx")
    ctx.wait(g, "ldx", nix)
    ctx.barrier()
    load_src_fm(ctx, scr["ln"])
    load_weights_bf(ctx, Wq, D, 1024)
    emit_proj_fm(ctx, scr["qf"], 1024)
    load_weights_bf(ctx, Wk, D, 1024)
    emit_proj_fm(ctx, scr["kf"], 1024)
    load_weights_bf(ctx, Wv, D, 1024)
    emit_proj_tm(ctx, scr["vt"], 1024, aug65=True)

    DIAG = scr["t5diag"]
    DCUR = scr["t5dcur"]
    DFLAT = DCUR.rearrange("a (w b) -> (a w) b", b=1)
    fwd_classes = [-512, -384, -256, -128, 0, 128]
    rev_classes = [-128, 0, 128, 256, 384, 512]

    def per_head(c2, kvh):
        nc2 = c2.nc
        g2, s2, t2, v2 = nc2.gpsimd, nc2.scalar, nc2.tensor, nc2.vector
        DG4 = DIAG.rearrange("(a h) (b w) -> a h b w", a=2, b=1)
        c2.dma(nc2.sync, DCUR[0:1, :], DG4[0, kvh], "ph")
        c2.dma(nc2.sync, DCUR[1:2, :], DG4[1, kvh], "ph")
        c2.wait(g2, "ph")
        for ci in range(12):
            c2.dma_i(g2, c2.maskb[:, ci * 512:(ci + 1) * 512], DFLAT,
                     c2.t5idx[:, ci:ci + 1], "ph")
        c2.dma(nc2.sync, c2.rbf[0:1, 8:9], DCUR[0:1, 0:1], "ph")
        c2.dma(nc2.sync, c2.rbf[0:1, 9:10], DCUR[0:1, 8190:8191], "ph")
        c2.wait(t2, "ph")
        ib = t2.matmul(c2.pb[5][:, 0:2], c2.onesrow[0:1, :], c2.rbf[0:1, 8:10],
                       start=True, stop=True)
        bb = c2.tick(ib, "ph2")
        c2.wait(s2, "ph2", bb)
        ib2 = s2.activation(c2.bvec[:, 0:2], c2.pb[5][:, 0:2], AF.Copy)
        c2.tick(ib2, "ph2")
        c2.wait(g2, "ph2")
        c2.wait(g2, "ph")
        c2.wait(v2, "ph")

    def hooks_factory(qb):
        def cls_a(kt, sub):
            dd = (kt - qb) * 512 - sub * 128
            if dd >= 255:
                return ("far", 1)
            if dd <= -639:
                return ("far", 0)
            return ("near", (dd + 512) // 128)

        def cls_b(kt, j):
            dd = (kt - qb) * 512 + j * 128
            if dd >= 639:
                return ("far", 1)
            if dd <= -255:
                return ("far", 0)
            return ("near", 6 + (dd + 128) // 128)

        def masksel(kt, i, phase):
            kind, val = cls_a(kt, i) if phase == "A" else cls_b(kt, i)
            return val if kind == "near" else None

        def abias(kt, sub):
            kind, val = cls_a(kt, sub)
            return ctx.bvec[:, val:val + 1] if kind == "far" else None

        def bbias(kt, j):
            kind, val = cls_b(kt, j)
            return ctx.bvec[:, val:val + 1] if kind == "far" else None
        return masksel, abias, bbias

    emit_attention(ctx, scr["qf"], scr["kf"], scr["vt"], scr["of"],
                   nheads=16, kv_group=1, mode="full",
                   hooks_factory=hooks_factory, per_head=per_head,
                   QCUR=scr["qcur"], OCUR=scr["ocur"])
    load_weights_bf(ctx, Wo, D, 1024)
    emit_out_proj_residual(ctx, scr["of"], XR)


def t5_host_tables():
    d = np.arange(-4095, 4096)
    nb = 16
    rel = (d > 0).astype(np.int64) * nb
    rp = np.abs(d)
    max_exact = nb // 2
    with np.errstate(divide="ignore"):
        rl = max_exact + (np.log(rp / max_exact) / math.log(128 / 8) * (nb - max_exact)).astype(np.int64)
    rl = np.minimum(rl, nb - 1)
    bucket = rel + np.where(rp < max_exact, rp, rl)
    oh_f = np.zeros((32, 8191), np.float32)
    oh_f[bucket, np.arange(8191)] = 1.0
    oh_r = oh_f[:, ::-1].copy()
    oh = np.concatenate([oh_f, oh_r], axis=1)
    i_ = np.arange(128)
    idx = np.zeros((128, 12), np.int32)
    for ci, dd in enumerate([-512, -384, -256, -128, 0, 128]):
        idx[:, ci] = 4095 + dd - i_
    for ci, dd in enumerate([-128, 0, 128, 256, 384, 512]):
        idx[:, 6 + ci] = 8192 + (4095 - dd - i_)
    return oh, idx




def emit_mask_load(ctx, CONSTS, which):
    nc = ctx.nc
    if which == "causal":
        ctx.dma(nc.gpsimd, ctx.maskb[:, 0:4096], CONSTS[:, 0:4096], "ldc")
    elif which == "anticausal":
        ctx.dma(nc.gpsimd, ctx.maskb[:, 0:4096], CONSTS[:, 4096:8192], "ldc")
    else:
        ctx.dma(nc.gpsimd, ctx.maskb[:], CONSTS[:, 0:8192], "ldc")
    ctx.wait(nc.vector, "ldc")
    ctx.barrier()


def emit_attention_big(ctx, QF, KF, VT, OF):
    nc = ctx.nc
    v, s, te, g = nc.vector, nc.scalar, nc.tensor, nc.gpsimd
    NQ = T // 512
    pe = ctx.sem("pe")
    scale = 1.0 / 32.0
    for qb in range(NQ):
        live512 = list(range(qb, NQ))
        live128 = list(range(qb * 4, NQ * 4))
        ctx.wait(nc.sync, "pe")
        nq = ctx.dma(nc.sync, ctx.q_sb[:].rearrange("p (c w) -> p c w", w=512),
                     QF.rearrange("(c p) (b w) -> p b c w", p=128, w=512)[:, qb], "ldx")
        for ki, kt in enumerate(live512):
            ctx.wait(nc.sync, "pe")
            nk = ctx.dma(nc.sync, ctx.k_sb[:].rearrange("p (c w) -> p c w", w=512),
                         KF.rearrange("(c p) (b w) -> p b c w", p=128, w=512)[:, kt], "ldx")
            ctx.wait(te, "ldx")
            ctx.wait(te, "ldx", nq)
            for sub in range(4):
                bank = ctx.pb[sub % 2]
                ctx.wait(te, "pe")
                last = None
                for c in range(8):
                    last = te.matmul(bank[:], ctx.q_sb[:, c * 512 + sub * 128:c * 512 + sub * 128 + 128],
                                     ctx.k_sb[:, c * 512:(c + 1) * 512],
                                     start=(c == 0), stop=(c == 7))
                mm = ctx.tick(last, "pe")
                ctx.wait(v, "pe", mm)
                i1 = v.tensor_single_scalar(ctx.y0[:, 0:512], bank[:], 0.0, ALU.is_equal)
                ctx.tick(i1, "pe"); ctx.wait(v, "pe")
                i2 = v.scalar_tensor_tensor(ctx.rows2[:], ctx.y0[:, 0:512], -9.0e15, bank[:],
                                            ALU.mult, ALU.add)
                ctx.tick(i2, "pe"); ctx.wait(v, "pe")
                if kt == qb:
                    i3 = v.tensor_add(ctx.rows2[:], ctx.rows2[:],
                                      ctx.maskb[:, sub * 512:(sub + 1) * 512])
                    ctx.tick(i3, "pe"); ctx.wait(v, "pe")
                ird = v.reduce_max(ctx.rows[:, 16 + sub * 12 + ki:17 + sub * 12 + ki],
                                   ctx.rows2[:], axis=AX.X)
                red = ctx.tick(ird, "pe")
                ctx.wait(te, "pe", red)
        for sub in range(4):
            ctx.wait(v, "pe")
            irf = v.reduce_max(ctx.rows[:, 8:9], ctx.rows[:, 16 + sub * 12:16 + sub * 12 + len(live512)],
                               axis=AX.X, negate=True)
            ctx.tick(irf, "pe"); ctx.wait(v, "pe")
            icp = v.tensor_copy(ctx.b0[:, 1040:1041], ctx.rows[:, 8:9])
            nm = ctx.tick(icp, "pe")
            ctx.wait(te, "pe", nm)
            itr = te.transpose(ctx.ptr[0:1, 0:128], ctx.b0[:, 1040:1041], ctx.ident[:])
            tr = ctx.tick(itr, "pe")
            ctx.wait(s, "pe", tr)
            isa = s.activation(ctx.rbf[0:1, sub * 128:sub * 128 + 128], ctx.ptr[0:1, 0:128],
                               AF.Copy)
            ctx.tick(isa, "pe")
        ctx.wait(te, "pe")
        for fi, k1 in enumerate(live128):
            bank = ctx.pb[2 + fi % 2]
            kt, j = k1 // 4, k1 % 4
            ctx.wait(nc.sync, "pe")
            nk = ctx.dma(nc.sync, ctx.b1[:, 0:1024].rearrange("p (c w) -> p c w", w=128),
                         KF.rearrange("(c p) (kk w) -> p kk c w", p=128, w=128)[:, k1], "ldx")
            ctx.wait(te, "ldx")
            last = None
            for c in range(8):
                last = te.matmul(bank[:], ctx.b1[:, c * 128:(c + 1) * 128],
                                 ctx.q_sb[:, c * 512:(c + 1) * 512].rearrange("p (a w) -> p a w", a=1)[:, 0]
                                 if False else ctx.q_sb[:, c * 512:c * 512 + 512],
                                 start=(c == 0), stop=(c == 7))
            mm = ctx.tick(last, "pe")
            imb = te.matmul(ctx.pb[5][:, 0:512], ctx.onesrow[0:1, 0:128], ctx.rbf[0:1, 0:512],
                            start=True, stop=True)
            mb = ctx.tick(imb, "pe")
            ctx.wait(v, "pe", mb)
            i1 = v.tensor_single_scalar(ctx.y0[:, 0:512], bank[:], 0.0, ALU.is_equal)
            ctx.tick(i1, "pe"); ctx.wait(v, "pe")
            i2 = v.scalar_tensor_tensor(ctx.rows2[:], ctx.y0[:, 0:512], -9.0e15, bank[:],
                                        ALU.mult, ALU.add)
            ctx.tick(i2, "pe"); ctx.wait(v, "pe")
            if kt == qb:
                i3 = v.tensor_add(ctx.rows2[:], ctx.rows2[:],
                                  ctx.maskb[:, (4 + j) * 512:(5 + j) * 512])
                ctx.tick(i3, "pe"); ctx.wait(v, "pe")
            i4 = v.tensor_add(ctx.rows2[:], ctx.rows2[:], ctx.pb[5][:, 0:512])
            mk = ctx.tick(i4, "pe")
            ctx.wait(s, "pe", mk)
            pslot = ctx.srcfm[fi % 8][:, (fi // 8) * 512:(fi // 8) * 512 + 512]
            iex = s.activation(pslot, ctx.rows2[:], AF.Exp)
            ctx.tick(iex, "pe")
        ctx.wait(s, "pe")
        ctx.wait(te, "pe")
        for rnd in range(2):
            for ki, k1 in enumerate(live128):
                ctx.wait(nc.sync, "pe")
                nk = ctx.dma(nc.sync, ctx.v_sb[:, 0:1024],
                             VT[k1 * 128:(k1 + 1) * 128, 0:1024], "ldx")
                ctx.wait(te, "ldx")
                pslot = ctx.srcfm[ki % 8][:, (ki // 8) * 512:(ki // 8) * 512 + 512]
                for dv in range(4):
                    dvc = rnd * 4 + dv
                    last = te.matmul(ctx.pb[dv][:], ctx.v_sb[:, dvc * 128:(dvc + 1) * 128],
                                     pslot, start=(ki == 0), stop=(ki == len(live128) - 1),
                                     skip_group_check=True)
                if rnd == 0:
                    last = te.matmul(ctx.pv[0:1, 0:512], ctx.onescol[:, 0:1], pslot,
                                     start=(ki == 0), stop=(ki == len(live128) - 1),
                                     skip_group_check=True)
                mm = ctx.tick(last, "pe")
                ctx.wait(te, "pe", mm)
            if rnd == 0:
                ctx.wait(v, "pe")
                ir = v.reciprocal(ctx.rows2[0:1, 0:512], ctx.pv[0:1, 0:512])
                ctx.tick(ir, "pe"); ctx.wait(v, "pe")
                ic = v.tensor_copy(ctx.rbf[0:1, 0:512], ctx.rows2[0:1, 0:512])
                rc = ctx.tick(ic, "pe")
                ctx.wait(te, "pe", rc)
                ib = te.matmul(ctx.ptr.bitcast(F32)[0:128, 0:512] if False else ctx.pb[5][:, 0:512],
                               ctx.onesrow[0:1, 0:128], ctx.rbf[0:1, 0:512],
                               start=True, stop=True)
                bb = ctx.tick(ib, "pe")
                ctx.wait(s, "pe", bb)
                isb = s.activation(ctx.b0[:, 0:512], ctx.pb[5][:, 0:512], AF.Copy)
                ctx.tick(isb, "pe")
            ctx.wait(v, "pe")
            for dv in range(4):
                dvc = rnd * 4 + dv
                ino = v.scalar_tensor_tensor(ctx.b1[:, 0:512], ctx.pb[dv][:], 1.0,
                                             ctx.b0[:, 0:512], ALU.mult, ALU.mult)
                no = ctx.tick(ino, "pe")
                ctx.wait(g, "pe", no)
                nst = ctx.dma(g, OF[dvc * 128:(dvc + 1) * 128, qb * 512:(qb + 1) * 512],
                              ctx.b1[:, 0:512], "st")
                ctx.wait(v, "st", nst)
            ctx.wait(te, "pe")
    ctx.barrier()


def emit_residual_from_fm(ctx, OF, XR, d=1024):
    nc = ctx.nc
    v = nc.vector
    pe = ctx.sem("pe")
    for t in range(T // 128):
        if t > 0:
            ctx.wait(nc.sync, "pe")
        n = ctx.dma(nc.sync, ctx.b0[:, 0:1024], OF[:, t * 128:(t + 1) * 128], "ldr",
                    transpose=True)
        n = ctx.dma(nc.sync, ctx.x0[:], XR[t * 128:(t + 1) * 128, :], "ldr")
        ctx.wait(v, "ldr", n)
        i1 = v.tensor_add(ctx.y0[:], ctx.x0[:], ctx.b0[:, 0:1024])
        ad = ctx.tick(i1, "pe")
        ctx.wait(nc.gpsimd, "pe", ad)
        nst = ctx.dma(nc.gpsimd, XR[t * 128:(t + 1) * 128, :], ctx.y0[:], "st")
        ctx.wait(v, "st", nst)
    ctx.barrier()


def emit_L1_attn_a(ctx, XR, W, CONSTS, scr):
    nc = ctx.nc
    emit_ln_tm(ctx, XR, scr["ln"])
    emit_mask_load(ctx, CONSTS, "anticausal")
    load_src_fm(ctx, scr["ln"])
    load_weights_bf(ctx, W[:, 0:1024], D, 1024)
    emit_proj_fm(ctx, scr["qf"], 1024, out_scale=1.0 / 32.0)
    load_weights_bf(ctx, W[:, 1024:2048], D, 1024)
    emit_proj_fm(ctx, scr["kf"], 1024)
    load_weights_bf(ctx, W[:, 2048:3072], D, 1024)
    emit_proj_tm(ctx, scr["vt"], 1024, aug65=False)
    emit_attention_big(ctx, scr["qf"], scr["kf"], scr["vt"], scr["of"])
    emit_residual_from_fm(ctx, scr["of"], XR)


def _l0a_elu1(ctx, src_bf, dst_bf, dst_f32):
    nc = ctx.nc
    v, s = nc.vector, nc.scalar
    i1 = v.tensor_scalar(ctx.x0[0:64, 0:512], src_bf, 0.0, 0.0, ALU.min, ALU.add)
    ctx.tick(i1); ctx.wait(s, "at")
    i2 = s.activation(ctx.y0[0:64, 0:512], ctx.x0[0:64, 0:512], AF.Exp)
    ctx.tick(i2)
    i3 = v.tensor_scalar(ctx.x1[0:64, 0:512], src_bf, 0.0, 0.0, ALU.max, ALU.add)
    ctx.tick(i3); ctx.wait(v, "at")
    i4 = v.tensor_add(dst_f32, ctx.y0[0:64, 0:512], ctx.x1[0:64, 0:512])
    ctx.tick(i4); ctx.wait(v, "at")
    i5 = v.tensor_copy(dst_bf, dst_f32)
    return ctx.tick(i5)


def _l0a_head(ctx, h, Q4, K4, V4, O4):
    nc = ctx.nc
    v, s, te, g = nc.vector, nc.scalar, nc.tensor, nc.gpsimd
    QU = ctx.srcfm[0][0:64, 0:512]
    SQ = ctx.srcfm[1][0:64, 0:512]
    SK = ctx.srcfm[2][0:64, 0:512]
    MhB = ctx.srcfm[4][0:64, 0:64]
    ZhB = ctx.srcfm[4][0:64, 64:65]
    QTM = ctx.srcfm[5][:, 0:256]
    KTM = ctx.srcfm[6][:, 0:256]
    ctx.barrier()
    ctx.dma(nc.sync, QTM.rearrange("p (j d) -> p j d", d=64), Q4[h], "ldkv")
    ctx.dma(nc.sync, KTM.rearrange("p (j d) -> p j d", d=64), K4[h], "ldkv")
    n1 = ctx.dma(nc.sync, ctx.v_sb[:, 0:260].rearrange("p (j c) -> p j c", c=65)[:, :, 0:64],
                 V4[h], "ldkv")
    ctx.wait(te, "ldkv", n1)
    for c in range(4):
        tq = te.transpose(ctx.ptr[0:64, c * 128:(c + 1) * 128],
                          QTM[:, c * 64:(c + 1) * 64], ctx.ident[:])
    for c in range(4):
        tq = te.transpose(ctx.ptr[0:64, 512 + c * 128:512 + (c + 1) * 128],
                          KTM[:, c * 64:(c + 1) * 64], ctx.ident[:])
    trk = ctx.tick(tq)
    ctx.wait(s, "at", trk)
    s.activation(QU, ctx.ptr[0:64, 0:512], AF.Copy)
    s.activation(ctx.q_sb[0:64, 0:512], ctx.ptr[0:64, 0:512], AF.Copy, scale=0.125)
    ia = s.activation(ctx.k_sb[0:64, 0:512], ctx.ptr[0:64, 512:1024], AF.Copy)
    qd = ctx.tick(ia)
    ctx.wait(te, "at")

    def msel(kt, i, phase):
        return i if phase == "A" else 4 + i
    _attn_qblock_body(ctx, 0, 0, None, None, [0], msel,
                      skip_qload=True, skip_ostore=True)
    ctx.wait(v, "at")
    _l0a_elu1(ctx, QU, SQ, ctx.y1[0:64, 0:512])
    i6 = s.activation(MhB, ctx.accm[0:64, h * 64:(h + 1) * 64], AF.Copy)
    ctx.tick(i6)
    i7 = s.activation(ZhB, ctx.accm[0:64, 1024 + h:1025 + h], AF.Copy)
    ctx.tick(i7)
    ctx.wait(te, "at")
    m1 = te.matmul(ctx.pb[0][0:64, 0:512], MhB, SQ, start=True, stop=True)
    ctx.tick(m1)
    m2 = te.matmul(ctx.pb[1][0:1, 0:512], ZhB, SQ, start=True, stop=True)
    d2 = ctx.tick(m2)
    ctx.wait(v, "at", d2)
    i8 = v.reciprocal(ctx.rows2[0:1, 0:512], ctx.pb[1][0:1, 0:512])
    ctx.tick(i8); ctx.wait(v, "at")
    i9 = v.tensor_copy(ctx.rbf[0:1, 0:512], ctx.rows2[0:1, 0:512])
    rcd = ctx.tick(i9)
    ctx.wait(te, "at", rcd)
    m3 = te.matmul(ctx.pb[2][0:64, 0:512], ctx.onesrow[0:1, 0:64], ctx.rbf[0:1, 0:512],
                   start=True, stop=True)
    bcd = ctx.tick(m3)
    ctx.wait(s, "at", bcd)
    i10 = s.activation(ctx.b0[0:64, 0:512], ctx.pb[2][0:64, 0:512], AF.Copy)
    bsd = ctx.tick(i10)
    ctx.wait(v, "at", bsd)
    i11 = v.scalar_tensor_tensor(ctx.x0[0:64, 0:512], ctx.pb[0][0:64, 0:512], 1.0,
                                 ctx.b0[0:64, 0:512], ALU.mult, ALU.mult)
    ctx.tick(i11); ctx.wait(v, "at")
    i12 = v.tensor_sub(ctx.y0[0:64, 0:512], ctx.x0[0:64, 0:512], ctx.b1[0:64, 0:512])
    ctx.tick(i12); ctx.wait(v, "at")
    i13 = v.tensor_scalar(ctx.y0[0:64, 0:512], ctx.y0[0:64, 0:512],
                          ctx.accm[0:64, 1040 + h:1041 + h], 0.0, ALU.mult, ALU.add)
    ctx.tick(i13); ctx.wait(v, "at")
    i14 = v.tensor_add(ctx.y0[0:64, 0:512], ctx.y0[0:64, 0:512], ctx.b1[0:64, 0:512])
    ctx.tick(i14); ctx.wait(v, "at")
    i15 = v.tensor_copy(ctx.b0[0:64, 0:512], ctx.y0[0:64, 0:512])
    atd = ctx.tick(i15)
    ctx.wait(te, "at", atd)
    for c in range(4):
        tr = te.transpose(ctx.ptr[:, c * 64:(c + 1) * 64],
                          ctx.b0[0:64, c * 128:(c + 1) * 128], ctx.ident[0:64, 0:64])
    trd = ctx.tick(tr)
    ctx.wait(s, "at", trd)
    i16 = s.activation(ctx.b1[:, 0:256], ctx.ptr[:, 0:256], AF.Copy)
    ost = ctx.tick(i16)
    ctx.wait(g, "at", ost)
    nst = ctx.dma(g, O4[h], ctx.b1[:, 0:256].rearrange("p (j d) -> p j d", d=64), "st")
    ctx.wait(v, "at")
    _l0a_elu1(ctx, ctx.k_sb[0:64, 0:512], SK, ctx.y1[0:64, 0:512])
    i22 = v.reduce_sum(ctx.rows[0:64, 40:41], ctx.y1[0:64, 0:512], axis=AX.X)
    ctx.tick(i22); ctx.wait(v, "at")
    i23 = v.tensor_add(ctx.accm[0:64, 1024 + h:1025 + h],
                       ctx.accm[0:64, 1024 + h:1025 + h], ctx.rows[0:64, 40:41])
    ctx.tick(i23)
    ctx.wait(te, "at")
    for c in range(4):
        tr2 = te.transpose(ctx.ptr[:, 256 + c * 64:256 + (c + 1) * 64],
                           SK[:, c * 128:(c + 1) * 128], ctx.ident[0:64, 0:64])
    t2d = ctx.tick(tr2)
    ctx.wait(s, "at", t2d)
    i24 = s.activation(ctx.srcfm[3][:, 0:256], ctx.ptr[:, 256:512], AF.Copy)
    s2d = ctx.tick(i24)
    ctx.wait(te, "at", s2d)
    for c in range(4):
        m4 = te.matmul(ctx.pb[3][0:64, 0:64], ctx.srcfm[3][:, c * 64:(c + 1) * 64],
                       ctx.v_sb[:, c * 65:c * 65 + 64], start=(c == 0), stop=(c == 3))
    dmd = ctx.tick(m4)
    ctx.wait(v, "at", dmd)
    i25 = v.tensor_add(ctx.accm[0:64, h * 64:(h + 1) * 64],
                       ctx.accm[0:64, h * 64:(h + 1) * 64], ctx.pb[3][0:64, 0:64])
    ctx.tick(i25)
    ctx.wait(g, "st", nst)


def emit_L0_attn_a(ctx, XR, Wq, Wk, Wv, Wo, BET, M0, Z0, CONSTS, scr,
                   nseg=8, nheads=16, do_seg=True):
    nc = ctx.nc
    v, s, te, g = nc.vector, nc.scalar, nc.tensor, nc.gpsimd
    emit_ln_tm(ctx, XR, scr["ln"])
    emit_mask_load(ctx, CONSTS, "causal")
    load_src_fm(ctx, scr["ln"])
    load_weights_bf(ctx, Wq, D, 1024)
    emit_proj_tm(ctx, scr["qtm"], 1024)
    load_weights_bf(ctx, Wk, D, 1024)
    emit_proj_tm(ctx, scr["ktm"], 1024)
    load_weights_bf(ctx, Wv, D, 1024)
    emit_proj_tm(ctx, scr["vtm"], 1024)
    ctx.dma(nc.sync, ctx.accm[0:64, 0:1024].rearrange("d (h w) -> d h w", w=64),
            M0.rearrange("(h d) w -> d h w", d=64), "ldx")
    ctx.dma(nc.sync, ctx.accm[0:64, 1024:1040], Z0, "ldx")
    n = ctx.dma(nc.sync, ctx.x1[0:64, 0:16], BET, "ldx")
    ctx.wait(s, "ldx", n)
    isg = s.activation(ctx.accm[0:64, 1040:1056], ctx.x1[0:64, 0:16], AF.Sigmoid)
    ctx.tick(isg, "pe"); ctx.wait(s, "pe")
    ctx.barrier()
    v.memset(ctx.k_sb[64:65, 0:512], 1.0)
    iv2 = v.memset(ctx.v_sb[:, 0:260].rearrange("p (j c) -> p j c", c=65)[:, :, 64:65], 1.0)
    ctx.tick(iv2, "pe"); ctx.wait(v, "pe")
    ctx.barrier()
    load_weights_bf(ctx, Wo, D, 1024)
    QTMs = scr["qtm"].rearrange("(a u) c -> a (u c)", u=512)
    KTMs = scr["ktm"].rearrange("(a u) c -> a (u c)", u=512)
    VTMs = scr["vtm"].rearrange("(a u) c -> a (u c)", u=512)
    ATMs = scr["atm"].rearrange("(a u) c -> a (u c)", u=512)
    Q4 = scr["qseg"].rearrange("u c -> (u c)").rearrange("(h j p d) -> h p j d",
                                                         j=4, p=128, d=64)
    K4 = scr["kseg"].rearrange("u c -> (u c)").rearrange("(h j p d) -> h p j d",
                                                         j=4, p=128, d=64)
    V4 = scr["vseg"].rearrange("u c -> (u c)").rearrange("(h j p d) -> h p j d",
                                                         j=4, p=128, d=64)
    O4 = scr["oseg"].rearrange("u c -> (u c)").rearrange("(h j p d) -> h p j d",
                                                         j=4, p=128, d=64)
    OSEGf = scr["oseg"].rearrange("u c -> (u c)")
    if do_seg:
        ctx.lc_begin({"at": 1600, "ldkv": 16 * 52, "st": 16 * 16})
        with nc.Fori(0, nseg) as sv:
            ctx.dma(nc.sync, scr["qseg"].rearrange("u c -> (u c)"), QTMs[sv], "ldkv")
            ctx.dma(nc.sync, scr["kseg"].rearrange("u c -> (u c)"), KTMs[sv], "ldkv")
            nsg = ctx.dma(nc.sync, scr["vseg"].rearrange("u c -> (u c)"), VTMs[sv], "ldkv")
            ctx.wait(nc.sync, "ldkv", nsg)
            for h in range(nheads):
                _l0a_head(ctx, h, Q4, K4, V4, O4)
            ctx.barrier()
            n2 = ctx.dma(nc.sync, ATMs[sv], OSEGf, "ldkv")
            ctx.wait(g, "ldkv", n2)
            ctx.body_end(["at", "ldkv", "st"])
        ctx.lc_end({"at": nseg, "ldkv": nseg, "st": nseg})
        ctx.barrier()
    load_src_fm(ctx, scr["atm"])
    emit_out_proj_residual(ctx, None, XR)


def emit_cast_shift(ctx, src_f32, dst_bf):
    nc = ctx.nc
    v = nc.vector
    for t in range(T // 128):
        r = t * 128
        if t > 0:
            ctx.wait(nc.sync, "lnm")
        n = ctx.dma(nc.sync, ctx.x0[:, 0:512], src_f32[r:r + 128, 0:512], "ldx")
        if t == 0:
            n = ctx.dma(nc.sync, ctx.x0[1:128, 512:1024], src_f32[0:127, 512:1024], "ldx")
        else:
            n = ctx.dma(nc.sync, ctx.x0[:, 512:1024], src_f32[r - 1:r + 127, 512:1024], "ldx")
        ctx.wait(v, "ldx", n)
        if t == 0:
            v.memset(ctx.x0[0:1, 512:1024], 0.0)
        ic = v.tensor_copy(ctx.b0[:, 0:1024], ctx.x0[:])
        ctx.tick(ic, "lnm")
        ctx.wait(nc.gpsimd, "lnm")
        nst = ctx.dma(nc.gpsimd, dst_bf[r:r + 128, :], ctx.b0[:, 0:1024], "st")
        ctx.wait(v, "st", nst)
    ctx.barrier()


def emit_proj_fm_l2xpos(ctx, dst_fm_ap, nheads, wcol0, qs_row_ap, tab_off):
    nc = ctx.nc
    v, s, te = nc.vector, nc.scalar, nc.tensor
    nq = ctx.dma(nc.gpsimd, ctx.rbf[0:1, 32:96], qs_row_ap, "ldx")
    ctx.wait(te, "ldx", nq)
    it = te.transpose(ctx.ptr[0:64, 0:1], ctx.rbf[0:1, 32:96], ctx.ident[0:1, 0:1])
    tt = ctx.tick(it, "pe")
    ctx.wait(s, "pe", tt)
    isq = s.activation(ctx.rows[0:64, 60:61], ctx.ptr[0:64, 0:1], AF.Copy)
    ctx.tick(isq, "pe")
    ctx.wait(te, "ldx")
    ctx.wait(te, "ldw")
    ctx.wait(te, "ldc")
    for h in range(nheads):
        for tb in range(T // 512):
            tc = slice(tb * 512, (tb + 1) * 512)
            bank = ctx.pb[(h * 8 + tb) % 2]
            rbank = ctx.pb[2 + (h * 8 + tb) % 2]
            ctx.wait(te, "pe")
            last = None
            for c in range(8):
                last = te.matmul(bank[0:64, :], ctx.wbuf[c][:, wcol0 + h * 64:wcol0 + h * 64 + 64],
                                 ctx.srcfm[c][:, tc], start=(c == 0), stop=(c == 7))
            mm = ctx.tick(last, "pe")
            ctx.wait(s, "pe", mm)
            isqr = s.activation(ctx.b1[0:64, 0:512], bank[0:64, :], AF.Square)
            sq = ctx.tick(isqr, "pe")
            ctx.wait(te, "pe", sq)
            iss = te.matmul(ctx.pb[4][0:1, 0:512], ctx.onescol[0:64, 0:1], ctx.b1[0:64, 0:512],
                            start=True, stop=True)
            ssd = ctx.tick(iss, "pe")
            ctx.wait(s, "pe", ssd)
            i1 = s.activation(ctx.rows2[0:1, 0:512], ctx.pb[4][0:1, 0:512], AF.Ln)
            ctx.tick(i1, "pe"); ctx.wait(s, "pe")
            i2 = s.activation(ctx.rows2[0:1, 0:512], ctx.rows2[0:1, 0:512], AF.Exp, scale=0.5)
            ctx.tick(i2, "pe"); ctx.wait(v, "pe")
            i3 = v.tensor_scalar_max(ctx.rows2[0:1, 0:512], ctx.rows2[0:1, 0:512], 1e-12)
            ctx.tick(i3, "pe"); ctx.wait(v, "pe")
            i4 = v.reciprocal(ctx.rows2[0:1, 0:512], ctx.rows2[0:1, 0:512])
            ctx.tick(i4, "pe"); ctx.wait(v, "pe")
            i5 = v.tensor_copy(ctx.rbf[0:1, 0:512], ctx.rows2[0:1, 0:512])
            rc = ctx.tick(i5, "pe")
            ctx.wait(te, "pe", rc)
            ib = te.matmul(ctx.pb[5][0:64, 0:512], ctx.onesrow[0:1, 0:64], ctx.rbf[0:1, 0:512],
                           start=True, stop=True)
            bb = ctx.tick(ib, "pe")
            ctx.wait(s, "pe", bb)
            isb = s.activation(ctx.y1[0:64, 0:512], ctx.pb[5][0:64, 0:512], AF.Copy)
            sb2 = ctx.tick(isb, "pe")
            ctx.wait(v, "pe", sb2)
            iqn = v.scalar_tensor_tensor(ctx.b1[0:64, 0:512], bank[0:64, :], ctx.rows[0:64, 60:61],
                                         ctx.y1[0:64, 0:512], ALU.mult, ALU.mult)
            qn = ctx.tick(iqn, "pe")
            ctx.wait(te, "pe", qn)
            irt = te.matmul(rbank[0:64, :], ctx.prot[:], ctx.b1[0:64, 0:512], start=True, stop=True)
            rt = ctx.tick(irt, "pe")
            ctx.wait(v, "pe", rt)
            v.tensor_mul(ctx.rows2[0:64, :], ctx.b1[0:64, 0:512],
                         ctx.maskb[0:64, tab_off + tb * 512:tab_off + (tb + 1) * 512])
            iv2 = v.tensor_mul(ctx.y1[0:64, 0:512], rbank[0:64, :],
                               ctx.maskb[0:64, tab_off + T + tb * 512:tab_off + (tb + 1) * 512 + T])
            ctx.tick(iv2, "pe"); ctx.wait(v, "pe")
            iv3 = v.tensor_add(ctx.b0[0:64, 0:512], ctx.rows2[0:64, :], ctx.y1[0:64, 0:512])
            ev = ctx.tick(iv3, "pe")
            ctx.wait(nc.gpsimd, "pe", ev)
            n = ctx.dma(nc.gpsimd, dst_fm_ap[h * 64:h * 64 + 64, tc], ctx.b0[0:64, 0:512], "st")
            ctx.wait(v, "st", n)
            ctx.wait(s, "st", n)
            if h == 0 and tb == 0 and getattr(emit_proj_fm_l2xpos, "_dbg", None) is not None:
                dd = emit_proj_fm_l2xpos._dbg
                ctx.wait(s, "pe")
                s.activation(ctx.rows2[0:64, :], bank[0:64, :], AF.Copy)
                i9 = s.activation(ctx.y1[0:64, 0:512], rbank[0:64, :], AF.Copy)
                ctx.tick(i9, "pe")
                ctx.wait(nc.gpsimd, "pe")
                ctx.dma(nc.gpsimd, dd[0], ctx.rows2[0:64, :], "st")
                ctx.dma(nc.gpsimd, dd[1], ctx.y1[0:64, 0:512], "st")
                ctx.dma(nc.gpsimd, dd[2], ctx.b1[0:64, 0:512], "st")
                ctx.dma(nc.gpsimd, dd[3], ctx.rows[0:64, 60:64], "st")
                nst2 = ctx.dma(nc.gpsimd, dd[4], ctx.rbf[0:1, 0:512], "st")
                ctx.wait(nc.gpsimd, "st", nst2)
    ctx.barrier()


def emit_geglu_w1(ctx, hn_tm, W1l, B1l, GEL, H4):
    nc = ctx.nc
    v, s, te, g = nc.vector, nc.scalar, nc.tensor, nc.gpsimd
    HID = 2730
    nch_out = 22
    load_src_fm(ctx, hn_tm)
    for half in (1, 0):
        ctx.wait(nc.gpsimd, "pe")
        if half == 0:
            ctx.wait(nc.sync, "st")
        load_weights_bf(ctx, W1l[:, half * HID:(half + 1) * HID], D, HID)
        ctx.wait(te, "ldw")
        for j in range(nch_out):
            rows = 128 if j < 21 else 42
            ctx.wait(nc.sync, "pe")
            nbj = ctx.dma(nc.sync, ctx.rows[0:rows, 62:63],
                          B1l[0:1, half * HID + j * 128:half * HID + j * 128 + rows]
                          .rearrange("a (r b) -> (a r) b", b=1), "ldx")
            ctx.wait(s, "ldx", nbj)
            ctx.wait(v, "ldx", nbj)
            for tb in range(T // 512):
                tc = slice(tb * 512, (tb + 1) * 512)
                bank = ctx.pb[(j * 8 + tb) % 2]
                ctx.wait(te, "pe")
                last = None
                for c in range(8):
                    last = te.matmul(bank[0:rows, :], ctx.wbuf[c][:, j * 128:j * 128 + rows],
                                     ctx.srcfm[c][:, tc], start=(c == 0), stop=(c == 7))
                mm = ctx.tick(last, "pe")
                ctx.wait(s, "pe", mm)
                if half == 1:
                    ie = s.activation(ctx.b0[0:rows, 0:512], bank[0:rows, :], AF.Gelu,
                                      bias=ctx.rows[0:rows, 62:63])
                    ev = ctx.tick(ie, "pe")
                    ctx.wait(g, "pe", ev)
                    n = ctx.dma(g, GEL[j * 128:j * 128 + rows, tc], ctx.b0[0:rows, 0:512], "st")
                    ctx.wait(s, "st", n)
                else:
                    ctx.wait(v, "pe", mm)
                    ie = v.tensor_scalar_add(ctx.rows2[0:rows, :], bank[0:rows, :],
                                             ctx.rows[0:rows, 62:63])
                    ev = ctx.tick(ie, "pe")
                    ctx.wait(nc.sync, "pe", ev)
                    ng = ctx.dma(nc.sync, ctx.b1[0:rows, 0:512], GEL[j * 128:j * 128 + rows, tc], "ldr")
                    ctx.wait(v, "ldr", ng)
                    ctx.wait(v, "pe", ev)
                    im = v.tensor_mul(ctx.b0[0:rows, 0:512], ctx.rows2[0:rows, :], ctx.b1[0:rows, 0:512])
                    ev2 = ctx.tick(im, "pe")
                    ctx.wait(g, "pe", ev2)
                    n = ctx.dma(g, H4[j * 128:j * 128 + rows, tc], ctx.b0[0:rows, 0:512], "st")
                    ctx.wait(v, "st", n)
                    ctx.wait(s, "st", n)
    ctx.barrier()


def emit_w2_residual(ctx, H4, W2l, B2l, src_f32, dst_f32):
    nc = ctx.nc
    v, s, te, g = nc.vector, nc.scalar, nc.tensor, nc.gpsimd
    for c in range(22):
        rows = 128 if c < 21 else 42
        if c == 21:
            nc.vector.memset(ctx.wbuf[c % 8][32:64, 1024 * (c // 8):1024 * (c // 8) + 1024], 0.0)
            im = nc.vector.memset(ctx.wbuf[c % 8][64:128, 1024 * (c // 8):1024 * (c // 8) + 1024], 0.0)
            ctx.tick(im, "pe")
            ctx.wait(g, "pe")
        ctx.dma(g, ctx.wbuf[c % 8][0:rows, 1024 * (c // 8):1024 * (c // 8) + 1024],
                W2l[c * 128:c * 128 + rows, :], "ldw")
    nb = ctx.dma(nc.sync, ctx.x1[:], B2l.broadcast_to((128, D)), "ldr")
    ctx.wait(te, "ldw")
    for t in range(T // 128):
        r = t * 128
        ctx.wait(nc.sync, "pe")
        nh = ctx.dma(nc.sync, ctx.k_sb[:, 0:2816].rearrange("p (c w) -> p c w", w=128),
                     H4.rearrange("(c p) (b w) -> p b c w", p=128, w=128)[:, t], "ldr")
        nx = ctx.dma(nc.sync, ctx.x0[:], src_f32[r:r + 128, :], "ldr")
        ctx.wait(te, "ldr", nh)
        ctx.wait(te, "pe")
        last = None
        for cg in range(2):
            bank = ctx.pb[cg]
            for c in range(22):
                last = te.matmul(bank[:], ctx.k_sb[:, c * 128:(c + 1) * 128],
                                 ctx.wbuf[c % 8][:, 1024 * (c // 8) + cg * 512:1024 * (c // 8) + cg * 512 + 512],
                                 start=(c == 0), stop=(c == 21))
        mm = ctx.tick(last, "pe")
        ctx.wait(v, "pe", mm)
        ctx.wait(v, "ldr", nx)
        v.tensor_add(ctx.y0[:, 0:512], ctx.pb[0][:], ctx.x0[:, 0:512])
        i1 = v.tensor_add(ctx.y0[:, 512:1024], ctx.pb[1][:], ctx.x0[:, 512:1024])
        ctx.tick(i1, "pe"); ctx.wait(v, "pe")
        i2 = v.tensor_add(ctx.y0[:], ctx.y0[:], ctx.x1[:])
        ad = ctx.tick(i2, "pe")
        ctx.wait(g, "pe", ad)
        nst = ctx.dma(g, dst_f32[r:r + 128, :], ctx.y0[:], "st")
        ctx.wait(v, "st", nst)
    ctx.barrier()


def emit_L1_ffn(ctx, XR, Wqkv, QS, KS, Wo, LNG, LNB, W1, B1, W2, B2, XPOS, CONSTS, scr):
    nc = ctx.nc
    emit_ln_tm(ctx, XR, scr["ln"], dst_f32_ap=scr["h0"])
    emit_mask_load(ctx, CONSTS, "window")
    im = nc.vector.memset(ctx.b0[:, 0:1024], 0.0)
    ctx.tick(im, "lnm"); ctx.wait(nc.gpsimd, "lnm")
    for k2 in range(4):
        n = ctx.dma(nc.gpsimd, scr["h4"][2730:2816, k2 * 1024:(k2 + 1) * 1024],
                    ctx.b0[0:86, 0:1024], "st")
    ctx.wait(nc.gpsimd, "st")
    Wqkv3 = Wqkv.rearrange("(l d) f -> l d f", l=2)
    Wo3 = Wo.rearrange("(l d) f -> l d f", l=2)
    W13 = W1.rearrange("(l d) f -> l d f", l=2)
    W23 = W2.rearrange("(l d) f -> l d f", l=2)
    B13 = B1.rearrange("(l a) f -> l a f", a=1)
    B23 = B2.rearrange("(l a) f -> l a f", a=1)
    LNG3 = LNG.rearrange("(l a) f -> l a f", a=1)
    LNB3 = LNB.rearrange("(l a) f -> l a f", a=1)
    QS3 = QS.rearrange("(l a) d -> l a d", a=1)
    KS3 = KS.rearrange("(l a) d -> l a d", a=1)
    budgets = {"pe": 1 << 14, "ldx": 16 << 11, "ldw": 16 << 9, "ldr": 16 << 9,
               "st": 16 << 11, "lnm": 1 << 10, "ldc": 16 * 16, "ldk": 16 * 4,
               "at": 2048, "aldk": 16, "ast": 16, "ldkv": 16 * 5}
    ctx.lc_begin(budgets)
    NQ = T // 512
    WS = scr["wl"]
    with nc.Fori(0, L1FFN_DEPTH) as lv:
        ctx.dma(nc.sync, WS["wqkv"], Wqkv3[lv], "ldc")
        ctx.dma(nc.sync, WS["wo"], Wo3[lv], "ldc")
        ctx.dma(nc.sync, WS["w1"], W13[lv], "ldc")
        ctx.dma(nc.sync, WS["w2"], W23[lv], "ldc")
        ctx.dma(nc.sync, WS["b1"], B13[lv], "ldc")
        ctx.dma(nc.sync, WS["lng"], LNG3[lv], "ldc")
        ctx.dma(nc.sync, WS["lnb"], LNB3[lv], "ldc")
        ctx.dma(nc.sync, WS["b2"], B23[lv], "ldc")
        ctx.dma(nc.sync, WS["qs"], QS3[lv], "ldc")
        ctx.dma(nc.sync, WS["ks"], KS3[lv], "ldc")
        ctx.wait(nc.sync, "ldc")
        ctx.wait(nc.gpsimd, "ldc")
        ctx.barrier()
        emit_cast_shift(ctx, scr["h0"], scr["ln2"])
        load_src_fm(ctx, scr["ln2"])
        for c in range(8):
            ctx.dma(nc.gpsimd, ctx.wbuf[c][:, 0:1536], WS["wqkv"][c * 128:(c + 1) * 128, :], "ldw")
        ctx.dma(nc.gpsimd, ctx.maskb[0:64, 0:8192], XPOS[:, 0:8192], "ldc")
        emit_proj_fm_l2xpos(ctx, scr["qf"], 8, 0, WS["qs"], 0)
        ctx.dma(nc.gpsimd, ctx.maskb[0:64, 0:8192], XPOS[:, 8192:16384], "ldc")
        emit_proj_fm_l2xpos(ctx, scr["kf"], 8, 512, WS["ks"], 0)
        emit_proj_tm(ctx, scr["vt"], 512, wcol0=1024, aug65=True)
        emit_mask_load(ctx, CONSTS, "window")
        emit_attention(ctx, scr["qf"], scr["kf"], scr["vt"], scr["of"],
                       nheads=8, kv_group=1, mode="window",
                       QCUR=scr["qcur"], OCUR=scr["ocur"], manage_lc=False)
        for c in range(4):
            ctx.dma(nc.gpsimd, ctx.wbuf[c][:, 0:1024], WS["wo"][c * 128:(c + 1) * 128, :], "ldw")
        emit_out_proj(ctx, scr["of"], scr["h0"], scr["h"], d_in=512)
        emit_ln_tm(ctx, scr["h"], scr["ln2"], w_ap=WS["lng"][0], b_ap=WS["lnb"][0], shift=True)
        emit_geglu_w1(ctx, scr["ln2"], WS["w1"], WS["b1"], scr["gel"], scr["h4"])
        emit_w2_residual(ctx, scr["h4"], WS["w2"], WS["b2"], scr["h"], scr["h0"])
        ctx.body_end(["pe", "ldx", "ldw", "ldr", "st", "lnm", "ldc", "ldk"])
    DEP = L1FFN_DEPTH
    nqb = DEP * 8 * NQ
    ctx.lc_end({"pe": DEP, "ldx": DEP, "ldw": DEP, "ldr": DEP, "st": DEP, "lnm": DEP,
                "ldc": DEP, "ldk": DEP, "at": nqb, "aldk": nqb, "ast": nqb, "ldkv": DEP * 8})
    emit_residual_add(ctx, XR, scr["h0"])


def emit_residual_add(ctx, XR, src_f32):
    nc = ctx.nc
    v = nc.vector
    for t in range(T // 128):
        if t > 0:
            ctx.wait(nc.sync, "pe")
        n = ctx.dma(nc.sync, ctx.x0[:], XR[t * 128:(t + 1) * 128, :], "ldr")
        n = ctx.dma(nc.sync, ctx.x1[:], src_f32[t * 128:(t + 1) * 128, :], "ldr")
        ctx.wait(v, "ldr", n)
        i1 = v.tensor_add(ctx.y0[:], ctx.x0[:], ctx.x1[:])
        ad = ctx.tick(i1, "pe")
        ctx.wait(nc.gpsimd, "pe", ad)
        nst = ctx.dma(nc.gpsimd, XR[t * 128:(t + 1) * 128, :], ctx.y0[:], "st")
        ctx.wait(v, "st", nst)
    ctx.barrier()



SUBLAYER_INPUTS = {
    "L0_attn_b": [("L0_attn_b_w_qkv", (D, 3072)), ("L0_attn_b_w_out", (D, D))],
}


def build_program(which, dbg=()):
    nc = bass.Bass("TRN2", target_bir_lowering=False)
    stack = contextlib.ExitStack()
    io = {}

    def inp(name, shape, dt=F32):
        if name not in io:
            io[name] = nc.dram_tensor(name, list(shape), dt, kind="ExternalInput").ap()
        return io[name]

    XR_in = inp("x_in", (T, D))
    CONSTS = inp("consts", (128, 9552))
    XR_out = nc.dram_tensor("x_out", [T, D], F32, kind="ExternalOutput").ap()
    scr = {
        "xr": nc.dram_tensor("scr_xr", [T, D], F32, kind="Internal").ap(),
        "ln": nc.dram_tensor("scr_ln", [T, D], BF16, kind="Internal").ap(),
        "qf": nc.dram_tensor("scr_qf", [D, T], BF16, kind="Internal").ap(),
        "kf": nc.dram_tensor("scr_kf", [D, T], BF16, kind="Internal").ap(),
        "vt": nc.dram_tensor("scr_vt", [T, 1040], BF16, kind="Internal").ap(),
        "of": nc.dram_tensor("scr_of", [D, T], BF16, kind="Internal").ap(),
        "t5diag": nc.dram_tensor("scr_t5diag", [32, 8192], BF16, kind="Internal").ap(),
        "h0": nc.dram_tensor("scr_h0", [T, D], F32, kind="Internal").ap(),
        "h": nc.dram_tensor("scr_h", [T, D], F32, kind="Internal").ap(),
        "ln2": nc.dram_tensor("scr_ln2", [T, D], BF16, kind="Internal").ap(),
        "qcur": nc.dram_tensor("scr_qcur", [256, T], BF16, kind="Internal").ap(),
        "hncur": nc.dram_tensor("scr_hncur", [512, D], BF16, kind="Internal").ap(),
        "hcur": nc.dram_tensor("scr_hcur", [512, D], F32, kind="Internal").ap(),
        "xcur": nc.dram_tensor("scr_xcur", [512, D], F32, kind="Internal").ap(),
        "gel": nc.dram_tensor("scr_gel", [2816, T], BF16, kind="Internal").ap(),
        "wl": {
            "wqkv": nc.dram_tensor("wl_wqkv", [D, 1536], F32, kind="Internal").ap(),
            "wo": nc.dram_tensor("wl_wo", [512, D], F32, kind="Internal").ap(),
            "w1": nc.dram_tensor("wl_w1", [D, 5460], F32, kind="Internal").ap(),
            "w2": nc.dram_tensor("wl_w2", [2730, D], F32, kind="Internal").ap(),
            "b1": nc.dram_tensor("wl_b1", [1, 5460], F32, kind="Internal").ap(),
            "lng": nc.dram_tensor("wl_lng", [1, D], F32, kind="Internal").ap(),
            "lnb": nc.dram_tensor("wl_lnb", [1, D], F32, kind="Internal").ap(),
            "b2": nc.dram_tensor("wl_b2", [1, D], F32, kind="Internal").ap(),
            "qs": nc.dram_tensor("wl_qs", [1, 64], F32, kind="Internal").ap(),
            "ks": nc.dram_tensor("wl_ks", [1, 64], F32, kind="Internal").ap(),
        },
        "h4": nc.dram_tensor("scr_h4", [2816, T], BF16, kind="Internal").ap(),
        "ocur": nc.dram_tensor("scr_ocur", [256, T], BF16, kind="Internal").ap(),
        "t5dcur": nc.dram_tensor("scr_t5dcur", [2, 8192], BF16, kind="Internal").ap(),
        "qtm": nc.dram_tensor("scr_qtm", [T, D], BF16, kind="Internal").ap(),
        "ktm": nc.dram_tensor("scr_ktm", [T, D], BF16, kind="Internal").ap(),
        "vtm": nc.dram_tensor("scr_vtm", [T, D], BF16, kind="Internal").ap(),
        "atm": nc.dram_tensor("scr_atm", [T, D], BF16, kind="Internal").ap(),
        "qseg": nc.dram_tensor("scr_qseg", [512, D], BF16, kind="Internal").ap(),
        "kseg": nc.dram_tensor("scr_kseg", [512, D], BF16, kind="Internal").ap(),
        "vseg": nc.dram_tensor("scr_vseg", [512, D], BF16, kind="Internal").ap(),
        "oseg": nc.dram_tensor("scr_oseg", [512, D], BF16, kind="Internal").ap(),
    }
    build_program._dbg_moe = None
    if "moe" in dbg:
        w_dbg = nc.dram_tensor("dbg_moew", [8, 512], BF16, kind="ExternalOutput").ap()
        a_dbg = nc.dram_tensor("dbg_moea", [128, 4096], F32, kind="ExternalOutput").ap()
        e_dbg = nc.dram_tensor("dbg_eacc", [1024, 4096], F32, kind="ExternalOutput").ap()
        build_program._dbg_moe = (w_dbg, a_dbg, e_dbg)
        dbg = [k2 for k2 in dbg if k2 != "moe"]
    with stack:
        ctx = Ctx(nc, stack)
        load_consts(ctx, CONSTS)
        ctx.dma(nc.sync, scr["xr"], XR_in, "ldx")
        ctx.wait(nc.sync, "ldx")
        ctx.barrier()
        XR = scr["xr"]
        for name in which:
            if name == "ln_only":
                emit_ln_tm(ctx, XR, scr["ln"])
            elif name == "qkv_only":
                emit_ln_tm(ctx, XR, scr["ln"])
                load_src_fm(ctx, scr["ln"])
                load_weights_bf(ctx, inp("L0_attn_b_w_qkv", (D, 3072))[:, 0:1024], D, 1024)
                emit_proj_fm(ctx, scr["qf"], 1024, wcol0=0, out_scale=1.0 / 32.0)
            elif name == "L0_attn_b":
                emit_L0_attn_b(ctx, XR,
                               inp("L0_attn_b_w_qkv", (D, 3072)),
                               inp("L0_attn_b_w_out", (D, D)), scr)
            elif name == "L0_ffn":
                emit_L0_ffn(ctx, XR,
                            inp("L0_ffn_ln1_w", (D,)), inp("L0_ffn_ln2_w", (D,)),
                            inp("L0_ffn_Wq", (D, D)), inp("L0_ffn_Wk", (D, 256)),
                            inp("L0_ffn_Wv", (D, 256)), inp("L0_ffn_Wo", (D, D)),
                            inp("L0_ffn_Wr", (D, 8)),
                            inp("L0_ffn_Wg", (8 * D, 2048)), inp("L0_ffn_Wu", (8 * D, 2048)),
                            inp("L0_ffn_Wd", (8 * 2048, D)),
                            inp("rope", (64, 16384)), CONSTS, scr)
            elif name == "L1_ffn":
                emit_L1_ffn(ctx, XR,
                            inp("L1_ffn_wqkv", (2 * D, 1536)), inp("L1_ffn_q_scale", (2, 64)),
                            inp("L1_ffn_k_scale", (2, 64)), inp("L1_ffn_wo", (2 * 512, D)),
                            inp("L1_ffn_ln_g", (2, D)), inp("L1_ffn_ln_b", (2, D)),
                            inp("L1_ffn_w1", (2 * D, 5460)), inp("L1_ffn_b1", (2, 5460)),
                            inp("L1_ffn_w2", (2 * 2730, D)), inp("L1_ffn_b2", (2, D)),
                            inp("xpos", (64, 4 * T)), CONSTS, scr)
            elif name == "L0_attn_a":
                emit_L0_attn_a(ctx, XR,
                               inp("L0_attn_a_Wq", (D, 1024)), inp("L0_attn_a_Wk", (D, 1024)),
                               inp("L0_attn_a_Wv", (D, 1024)), inp("L0_attn_a_Wo", (1024, D)),
                               inp("L0_attn_a_betas", (64, 16)),
                               inp("L0_attn_a_memory0", (1024, 64)),
                               inp("L0_attn_a_z0", (64, 16)), CONSTS, scr,
                               **getattr(build_program, "_l0a", {}))
            elif name == "L1_attn_a":
                emit_L1_attn_a(ctx, XR, inp("L1_attn_a_W", (D, 3072)), CONSTS, scr)
            elif name == "L1_attn_b":
                emit_L1_attn_b(ctx, XR,
                               inp("L1_attn_b_Wq", (D, D)), inp("L1_attn_b_Wk", (D, D)),
                               inp("L1_attn_b_Wv", (D, D)), inp("L1_attn_b_Wo", (D, D)),
                               inp("t5oh", (32, 16382)), inp("t5idx", (128, 12), I32),
                               inp("L1_attn_b_rel_bias", (32, 16)), scr)
            else:
                raise KeyError(name)
        for k2 in dbg:
            dt2 = scr[k2].dtype
            o2 = nc.dram_tensor(f"dbg_{k2}", list(scr[k2].shape), dt2, kind="ExternalOutput").ap()
            n = ctx.dma(nc.sync, o2, scr[k2], "st")
        n = ctx.dma(nc.sync, XR_out, XR, "st")
        ctx.wait(nc.sync, "st", n)
        ctx.barrier()
    return nc


FULL_ORDER = ["L0_attn_a", "L0_ffn", "L0_attn_b", "L0_ffn",
              "L1_attn_a", "L1_ffn", "L1_attn_b", "L1_ffn"]


def kernel(**inputs):
    f32 = np.float32

    def gi(k):
        return np.ascontiguousarray(np.asarray(inputs[k], dtype=f32))

    xs = gi("L0_attn_a_x")
    oh, idx = t5_host_tables()
    common = {
        "consts": host_consts(), "rope": host_rope_scaled(), "xpos": host_xpos(),
        "t5oh": oh, "t5idx": idx,
        "L0_attn_a_Wq": gi("L0_attn_a_Wq"), "L0_attn_a_Wk": gi("L0_attn_a_Wk"),
        "L0_attn_a_Wv": gi("L0_attn_a_Wv"), "L0_attn_a_Wo": gi("L0_attn_a_Wo"),
        "L0_attn_a_betas": np.ascontiguousarray(gi("L0_attn_a_betas").reshape(16, 64).T),
        "L0_attn_a_memory0": gi("L0_attn_a_memory0").reshape(1024, 64),
        "L0_attn_a_z0": np.ascontiguousarray(gi("L0_attn_a_z0").reshape(16, 64).T),
        "L0_attn_b_w_qkv": gi("L0_attn_b_w_qkv"), "L0_attn_b_w_out": gi("L0_attn_b_w_out"),
        "L0_ffn_ln1_w": gi("L0_ffn_ln1_w"), "L0_ffn_ln2_w": gi("L0_ffn_ln2_w"),
        "L0_ffn_Wq": gi("L0_ffn_Wq"), "L0_ffn_Wk": gi("L0_ffn_Wk"),
        "L0_ffn_Wv": gi("L0_ffn_Wv"), "L0_ffn_Wo": gi("L0_ffn_Wo"),
        "L0_ffn_Wr": gi("L0_ffn_Wr"),
        "L0_ffn_Wg": gi("L0_ffn_Wg").reshape(8 * 1024, 2048),
        "L0_ffn_Wu": gi("L0_ffn_Wu").reshape(8 * 1024, 2048),
        "L0_ffn_Wd": gi("L0_ffn_Wd").reshape(8 * 2048, 1024),
        "L1_attn_a_W": gi("L1_attn_a_W"),
        "L1_attn_b_Wq": gi("L1_attn_b_Wq"), "L1_attn_b_Wk": gi("L1_attn_b_Wk"),
        "L1_attn_b_Wv": gi("L1_attn_b_Wv"), "L1_attn_b_Wo": gi("L1_attn_b_Wo"),
        "L1_attn_b_rel_bias": gi("L1_attn_b_rel_bias"),
        "L1_ffn_wqkv": gi("L1_ffn_wqkv").reshape(2048, 1536),
        "L1_ffn_q_scale": gi("L1_ffn_q_scale"), "L1_ffn_k_scale": gi("L1_ffn_k_scale"),
        "L1_ffn_wo": gi("L1_ffn_wo").reshape(1024, 1024),
        "L1_ffn_ln_g": gi("L1_ffn_ln_g"), "L1_ffn_ln_b": gi("L1_ffn_ln_b"),
        "L1_ffn_w1": gi("L1_ffn_w1").reshape(2048, 5460),
        "L1_ffn_b1": gi("L1_ffn_b1"),
        "L1_ffn_w2": gi("L1_ffn_w2").reshape(2 * 2730, 1024),
        "L1_ffn_b2": gi("L1_ffn_b2"),
    }
    xcur = xs
    for name in FULL_ORDER:
        nc = build_program([name])
        in_maps = [dict(common, x_in=np.ascontiguousarray(xcur[b])) for b in range(8)]
        res = run_bass_kernel_spmd(nc, in_maps, core_ids=list(range(8)))
        xcur = np.stack([np.asarray(res.results[b]["x_out"], dtype=f32)
                         for b in range(8)])
    return xcur
